# version 01
# baseline (speedup 1.0000x reference)
import math
import numpy as np
from contextlib import ExitStack

import concourse.bass as bass
import concourse.mybir as mybir
from concourse.bass import IndirectOffsetOnAxis
from concourse.bass_utils import run_bass_kernel_spmd

F32 = mybir.dt.float32
BF16 = mybir.dt.bfloat16
U32 = mybir.dt.uint32
AF = mybir.ActivationFunctionType
ALU = mybir.AluOpType
AX = mybir.AxisListType

ENGINES = ("sync", "tensor", "vector", "scalar", "gpsimd")
B_PER_CORE = 4
T = 2048
D = 1024
NTOK = B_PER_CORE * T
NTILE = NTOK // 128
LN_EPS = 1e-5
S7_PHASES = 99
S7_NB = B_PER_CORE
S1_PHASES = 99


def alibi_slopes_np(num_heads):
    x = (2.0 ** 8) ** (1.0 / num_heads)
    return np.asarray([1.0 / x ** (i + 1) for i in range(num_heads)], dtype=np.float32)


def host_consts():
    c = {}
    m = np.zeros((4, 128, 512), dtype=np.float32)
    for j in range(4):
        s = 128 * j + np.arange(128)[:, None]
        t = np.arange(512)[None, :]
        m[j] = (s <= t).astype(np.float32)
    c["cmask512"] = m.reshape(512, 512)
    c["lt128"] = (np.arange(128)[:, None] < np.arange(128)[None, :]).astype(np.float32)
    c["ge128"] = (np.arange(128)[:, None] >= np.arange(128)[None, :]).astype(np.float32)
    c["iota8"] = np.tile(np.arange(8, dtype=np.float32), (128, 1))
    c["iota32"] = np.tile(np.arange(32, dtype=np.float32), (128, 1))
    slopes = alibi_slopes_np(16)
    ali = np.zeros((128, 256), dtype=np.float32)
    for h in range(16):
        for sc in range(16):
            ali[:, 16 * h + sc] = slopes[h] * (128.0 * sc + np.arange(128) - 2047.0)
    c["alibi"] = ali
    c["ID128"] = np.eye(128, dtype=np.float32)
    return c


class Sem:
    def __init__(self, nc, es, name):
        self.h = es.enter_context(nc.semaphore(name))
        self.v = 0


class Ctx:
    _UID = [0]

    def __init__(self, nc, es):
        self.nc = nc
        self.es = es
        self.sems = {}
        for name in ("pe", "dve", "act", "gp", "aux"):
            self.sems[name] = Sem(nc, es, "S_" + name)
        self.pipe = [Sem(nc, es, f"P_{i}") for i in range(48)]

    def sub(self, es):
        s = Ctx.__new__(Ctx)
        s.__dict__.update(self.__dict__)
        s.es = es
        return s

    def sb(self, name, shape, dtype):
        Ctx._UID[0] += 1
        return self.es.enter_context(
            self.nc.sbuf_tensor(f"{name}_{Ctx._UID[0]}", shape, dtype))

    def ps(self, name, shape, dtype=F32):
        Ctx._UID[0] += 1
        return self.es.enter_context(
            self.nc.psum_tensor(f"{name}_{Ctx._UID[0]}", shape, dtype))

    def bc_reg(self, e, val):
        r = getattr(e, "_kbc_reg", None)
        if r is None:
            r = e.alloc_register("bcreg")
            e.reg_mov(r, val)
            e._kbc_reg = r
        return r


class Script:
    def __init__(self, ctx):
        self.ctx = ctx
        self.ops = {e: [] for e in ENGINES}

    _DRAIN_ENGINES = ("vector", "scalar", "gpsimd")

    def op(self, eng, fn, drain=None):
        if drain is None:
            drain = eng in Script._DRAIN_ENGINES

        def run(e, fn=fn, drain=drain):
            fn(e)
            if drain:
                e.drain()
        self.ops[eng].append(run)

    def _sem(self, sem):
        return sem if isinstance(sem, Sem) else self.ctx.sems[sem]

    def inc(self, eng, fn, sem, k=1, drain=None):
        if drain is None:
            drain = eng in Script._DRAIN_ENGINES
        s = self._sem(sem)
        s.v += k
        h = s.h

        def run(e, fn=fn, h=h, k=k, drain=drain):
            fn(e).then_inc(h, k)
            if drain:
                e.drain()
        self.ops[eng].append(run)
        return s.v

    def wait(self, eng, sem, val):
        if val <= 0:
            return
        h = self._sem(sem).h

        def run(e, h=h, val=val):
            e.wait_ge(h, val)
        self.ops[eng].append(run)

    def now(self, sem):
        return self._sem(sem).v

    def barrier(self, engines=ENGINES):
        for p in self.ctx.pipe:
            if p.v:
                self.wait("gpsimd", p, p.v)
        s = self.ctx.sems["aux"]
        tgt = s.v + len(engines)
        for e in engines:
            h = s.h

            def run(eng, h=h, tgt=tgt):
                eng.sem_inc(h, 1)
                eng.wait_ge(h, tgt)
            self.ops[e].append(run)
        s.v = tgt

    def emit(self):
        nc = self.ctx.nc
        with nc.Block() as blk:
            for e in ENGINES:
                if not self.ops[e]:
                    continue
                ops = self.ops[e]

                def section(eng, ops=ops):
                    for f in ops:
                        f(eng)
                getattr(blk, e)(section)
        self.ops = {e: [] for e in ENGINES}

    def dma(self, queue, sem, out, in_, transpose=False, slow_ok=False):
        if transpose:
            assert queue in ("sync", "scalar")
            return self.inc(queue,
                            lambda e: e.dma_start(out=out, in_=in_, transpose=True),
                            sem, 16, drain=False)
        return self.inc(queue, lambda e: e.dma_start(
            out=out, in_=in_, allow_slow_non_contiguous=slow_ok), sem, 16,
            drain=False)


def ceil_div(a, b):
    return (a + b - 1) // b


def plan_ln(S, ctx, r_ap, xn_ap, ntile=NTILE, pipe0=0):
    rb = [ctx.sb("ln_r", [128, D], F32) for _ in range(2)]
    xb = [ctx.sb("ln_x", [128, D], BF16) for _ in range(2)]
    sq = ctx.sb("ln_sq", [128, D], F32)
    st = ctx.sb("ln_st", [128, 8], F32)
    ld = [ctx.pipe[pipe0], ctx.pipe[pipe0 + 1]]
    sto = [ctx.pipe[pipe0 + 2], ctx.pipe[pipe0 + 3]]
    t_load = [0] * ntile
    a1 = [0] * ntile
    v2 = [0] * ntile
    st_store = [0] * ntile
    for i in range(ntile):
        m = i % 2
        if i >= 2:
            S.wait("sync", "dve", v2[i - 2])
            S.wait("sync", "act", a1[i - 2])
        t_load[i] = S.dma("sync", ld[m], rb[m][:], r_ap[128 * i:128 * (i + 1), :])
        S.wait("scalar", ld[m], t_load[i])
        a1[i] = S.inc("scalar", lambda e, m=m: e.activation(
            sq[:], rb[m][:], AF.Square, accum_out=st[:, 1:2]), "act")
        S.wait("vector", ld[m], t_load[i])
        S.op("vector", lambda e, m=m: e.tensor_reduce(st[:, 0:1], rb[m][:], AX.X, ALU.add))
        S.wait("vector", "act", a1[i])
        S.op("vector", lambda e: e.tensor_scalar_mul(st[:, 2:3], st[:, 0:1], 1.0 / D))
        S.op("vector", lambda e: e.tensor_scalar_mul(st[:, 3:4], st[:, 1:2], 1.0 / D))
        S.op("vector", lambda e: e.tensor_mul(st[:, 4:5], st[:, 2:3], st[:, 2:3]))
        S.op("vector", lambda e: e.tensor_sub(st[:, 5:6], st[:, 3:4], st[:, 4:5]))
        v1 = S.inc("vector", lambda e: e.tensor_scalar_add(st[:, 5:6], st[:, 5:6], LN_EPS), "dve")
        S.wait("scalar", "dve", v1)
        a2 = S.inc("scalar", lambda e: e.activation(
            st[:, 6:7], st[:, 5:6], AF.Sqrt), "act")
        S.wait("vector", "act", a2)
        S.op("vector", lambda e: e.reciprocal(st[:, 7:8], st[:, 6:7]))
        if i >= 2:
            S.wait("vector", sto[m], st_store[i - 2])
        v2[i] = S.inc("vector", lambda e, m=m: e.tensor_scalar(
            xb[m][:], rb[m][:], st[:, 2:3], st[:, 7:8], ALU.subtract, ALU.mult), "dve")
        S.wait("gpsimd", "dve", v2[i])
        st_store[i] = S.dma("gpsimd", sto[m], xn_ap[128 * i:128 * (i + 1), :], xb[m][:])
    return st_store[ntile - 1]


def plan_load_w_chunks(S, sem, dst_tiles, w_ap, queue="gpsimd"):
    for kc in range(len(dst_tiles)):
        t = S.dma(queue, sem, dst_tiles[kc][:], w_ap[128 * kc:128 * (kc + 1), :])
    return t


def plan_load_xnt(S, sem, xnt, xn_ap, row0, nrows=T):
    for c in range(len(xnt)):
        t = S.dma("sync", sem, xnt[c][:, 0:nrows],
                  xn_ap[row0:row0 + nrows, 128 * c:128 * (c + 1)], transpose=True)
    return t


def plan_proj_fm(S, ctx, out_tiles, xnt, wtiles, w_sem, w_ready, x_sem, xnt_ready,
                 pbanks, bias_col=None, nk=8, ndc=8, tlen=T, prev_evac=0):
    ngr = tlen // 512
    evac_t = {}
    pe_t = 0
    for dc in range(ndc):
        for g in range(ngr):
            idx = dc * ngr + g
            bank = pbanks[idx % 2]
            if idx >= 2:
                S.wait("tensor", "act", evac_t[idx - 2])
            if idx == 0:
                S.wait("tensor", w_sem, w_ready)
                S.wait("tensor", x_sem, xnt_ready)
                S.wait("tensor", "act", prev_evac)
            for kc in range(nk - 1):
                S.op("tensor", lambda e, bank=bank, kc=kc, dc=dc, g=g: e.matmul(
                    bank[:], wtiles[kc][:, 128 * dc:128 * (dc + 1)],
                    xnt[kc][:, 512 * g:512 * (g + 1)], start=(kc == 0), stop=False))
            pe_t = S.inc("tensor", lambda e, bank=bank, dc=dc, g=g, kc=nk - 1: e.matmul(
                bank[:], wtiles[kc][:, 128 * dc:128 * (dc + 1)],
                xnt[kc][:, 512 * g:512 * (g + 1)], start=(nk == 1), stop=True), "pe")
            S.wait("scalar", "pe", pe_t)
            if bias_col is not None:
                bc = bias_col(dc)
                evac_t[idx] = S.inc("scalar", lambda e, bank=bank, dc=dc, g=g, bc=bc: e.add(
                    out_tiles[dc][:, 512 * g:512 * (g + 1)], bank[:], bc), "act")
            else:
                evac_t[idx] = S.inc("scalar", lambda e, bank=bank, dc=dc, g=g: e.copy(
                    out_tiles[dc][:, 512 * g:512 * (g + 1)], bank[:]), "act")
    return pe_t, evac_t[ndc * ngr - 1]


def plan_l1_attn_a(S, ctx, r_in, r_out, xn_ap, w, cst):
    with ExitStack() as es_ln:
        plan_ln(S, ctx.sub(es_ln), r_in, xn_ap, pipe0=0)
        S.barrier()
        S.emit()
    for b in range(B_PER_CORE):
        with ExitStack() as es_b:
            _l1aa_batch(S, ctx.sub(es_b), r_in, r_out, xn_ap, w, cst, b)
            S.barrier()
            S.emit()


def _l1aa_batch(S, ctx, r_in, r_out, xn_ap, w, cst, b):
    row0 = b * T
    xnt = [ctx.sb("aa_xnt", [128, T], BF16) for _ in range(8)]
    qt = [ctx.sb("aa_qt", [128, T], BF16) for _ in range(8)]
    kt = [ctx.sb("aa_kt", [128, T], BF16) for _ in range(8)]
    vt = [ctx.sb("aa_v", [128, 16 * 65], BF16) for _ in range(16)]
    wbuf = [ctx.sb("aa_w", [128, D], BF16) for _ in range(8)]
    wo = wbuf
    bqk = ctx.sb("aa_bqk", [128, 16], F32)
    browB = ctx.sb("aa_brow", [1, 2 * D], BF16)
    ones1b = ctx.sb("aa_o1b", [1, 128], BF16)
    ones1f = ctx.sb("aa_o1f", [1, 128], F32)
    ones2b = ctx.sb("aa_o2b", [2, 1], BF16)
    onesh = ctx.sb("aa_onesh", [128, 2], BF16)
    alib = ctx.sb("aa_alibi", [128, 256], F32)
    btab = ctx.sb("aa_btab", [128, 256], F32)
    sqb = ctx.sb("aa_sqb", [128, T], BF16)
    mx2 = ctx.sb("aa_mx2", [2, 24], F32)
    nrmb = ctx.sb("aa_nrmb", [2, 2], BF16)
    sca = ctx.sb("aa_sca", [1, 8], F32)
    expb = [ctx.sb("aa_exp", [128, 512], BF16) for _ in range(2)]
    dnum = ctx.sb("aa_dnum", [128, 512], BF16)
    drec = ctx.sb("aa_drec", [1, 512], F32)
    ctxt = [ctx.sb("aa_ctxt", [128, 512], BF16) for _ in range(8)]
    rb2 = [ctx.sb("aa_rb", [128, D], F32) for _ in range(2)]
    rnew = [ctx.sb("aa_rn", [128, D], F32) for _ in range(2)]

    pA = ctx.ps("aa_pA", [128, 512])
    pB = ctx.ps("aa_pB", [128, 512])
    pst = [ctx.ps("aa_pst", [128, 512]) for _ in range(2)]
    pnum = ctx.ps("aa_pnum", [128, 512])
    pbc = ctx.ps("aa_pbc", [128, 512])

    sx = ctx.pipe[8]
    sw = ctx.pipe[9]
    ss = ctx.pipe[10]
    sr = [ctx.pipe[11], ctx.pipe[12]]
    so = [ctx.pipe[13], ctx.pipe[14]]
    xnt_t = plan_load_xnt(S, sx, xnt, xn_ap, row0)
    wq_t = plan_load_w_chunks(S, sw, wbuf, w["Wq"])
    t_bqk = S.dma("sync", ss, bqk[:], w["bqk_cols"][:, :])
    t_brow = S.dma("sync", ss, browB[:], w["bvbo_row"][:, :])
    t_ali = S.dma("sync", ss, alib[:], cst["alibi"][:, :])
    S.op("vector", lambda e: e.memset(ones1b[:], 1.0))
    S.op("vector", lambda e: e.memset(ones1f[:], 1.0))
    S.op("vector", lambda e: e.memset(ones2b[:], 1.0))
    S.op("vector", lambda e: e.memset(onesh[:], 0.0))
    S.op("vector", lambda e: e.memset(onesh[0:64, 0:1], 1.0))
    S.op("vector", lambda e: e.memset(onesh[64:128, 1:2], 1.0))
    for st_ in range(16):
        S.op("vector", lambda e, st_=st_: e.memset(
            vt[st_][:].rearrange("p (h c) -> p h c", c=65)[:, :, 64:65], 1.0))
    v_setup = S.inc("vector", lambda e: e.memset(mx2[:, :], 0.0), "dve")

    S.wait("scalar", ss, t_ali)
    pe_q, ev_q = plan_proj_fm(S, ctx, qt, xnt, wbuf, sw, wq_t, sx, xnt_t, (pA, pB),
                              bias_col=lambda dc: bqk[:, dc:dc + 1])
    S.wait("gpsimd", "pe", pe_q)
    wk_t = plan_load_w_chunks(S, sw, wbuf, w["Wk"])
    pe_k, ev_k = plan_proj_fm(S, ctx, kt, xnt, wbuf, sw, wk_t, sx, xnt_t, (pA, pB),
                              bias_col=lambda dc: bqk[:, 8 + dc:9 + dc],
                              prev_evac=ev_q)
    S.wait("gpsimd", "pe", pe_k)
    wv_t = plan_load_w_chunks(S, sw, wbuf, w["Wv"])

    ev_v = [0] * 32
    for st_ in range(16):
        for half in range(2):
            idx = st_ * 2 + half
            bank = (pA, pB)[idx % 2]
            if idx >= 2:
                S.wait("tensor", "dve", ev_v[idx - 2])
            if idx == 0:
                S.wait("tensor", sw, wv_t)
                S.wait("tensor", ss, t_ali)
                S.wait("tensor", "dve", v_setup)
                S.wait("tensor", "act", ev_k)
            for kc in range(8):
                S.op("tensor", lambda e, bank=bank, kc=kc, st_=st_, half=half: e.matmul(
                    bank[:], xnt[kc][:, 128 * st_:128 * (st_ + 1)],
                    wbuf[kc][:, 512 * half:512 * (half + 1)],
                    start=(kc == 0), stop=False))
            pe_v = S.inc("tensor", lambda e, bank=bank, half=half: e.matmul(
                bank[:], ones1b[:], browB[0:1, 512 * half:512 * (half + 1)],
                start=False, stop=True), "pe")
            S.wait("vector", "pe", pe_v)
            ev_v[idx] = S.inc("vector", lambda e, bank=bank, st_=st_, half=half: e.tensor_copy(
                vt[st_][:, half * 520:(half + 1) * 520]
                .rearrange("p (h c) -> p h c", c=65)[:, :, 0:64],
                bank[:].rearrange("p (h d) -> p h d", d=64)), "dve")
    v_last = ev_v[31]
    S.wait("gpsimd", "pe", S.now("pe"))
    wo_t = plan_load_w_chunks(S, sw, wbuf, w["Wo"])

    S.wait("scalar", "act", max(ev_q, ev_k))
    banks3 = (pst[0], pst[1], pnum, pbc)
    v_red = v_setup
    pe3 = 0
    for i, src_tiles in enumerate((qt, kt)):
        for c in range(8):
            S.wait("scalar", "pe", pe3)
            a_sq2 = S.inc("scalar", lambda e, tt=src_tiles, c=c: e.activation(
                sqb[:], tt[c][:], AF.Square), "act")
            S.wait("tensor", "act", a_sq2)
            S.wait("tensor", "dve", v_red)
            for g in range(4):
                pe3 = S.inc("tensor", lambda e, g=g: e.matmul(
                    banks3[g][0:2, 0:512], onesh[:], sqb[:, 512 * g:512 * (g + 1)],
                    start=True, stop=True), "pe")
            S.wait("vector", "pe", pe3)
            for g in range(4):
                S.op("vector", lambda e, g=g: e.reduce_max(
                    mx2[0:2, 20 + g:21 + g], banks3[g][0:2, 0:512], AX.X))
            v_red = S.inc("vector", lambda e, i=i, c=c: e.reduce_max(
                mx2[0:2, 8 * i + c:8 * i + c + 1], mx2[0:2, 20:24], AX.X), "dve")
    S.op("vector", lambda e: e.reduce_max(mx2[0:2, 16:17], mx2[0:2, 0:8], AX.X))
    v_n = S.inc("vector", lambda e: e.reduce_max(
        mx2[0:2, 17:18], mx2[0:2, 8:16], AX.X), "dve")
    S.wait("scalar", "dve", v_n)
    a_nb = S.inc("scalar", lambda e: e.copy(nrmb[:], mx2[0:2, 16:18]), "act")
    S.wait("tensor", "act", a_nb)
    S.wait("tensor", "dve", v_n)
    pe_s = S.inc("tensor", lambda e: e.matmul(pbc[0:1, 16:18], ones2b[:], nrmb[:],
                                              start=True, stop=True), "pe")
    S.wait("scalar", "pe", pe_s)
    a_rt = S.inc("scalar", lambda e: e.activation(sca[0:1, 0:2], pbc[0:1, 16:18],
                                                  AF.Sqrt), "act")
    S.wait("vector", "act", a_rt)
    S.op("vector", lambda e: e.tensor_mul(sca[0:1, 2:3], sca[0:1, 0:1], sca[0:1, 1:2]))
    v_B = S.inc("vector", lambda e: e.tensor_scalar_mul(sca[0:1, 3:4], sca[0:1, 2:3],
                                                        -0.125), "dve")
    S.wait("tensor", "dve", v_B)
    pe_bc = S.inc("tensor", lambda e: e.matmul(pbc[:, 0:1], ones1f[:], sca[0:1, 3:4],
                                               start=True, stop=True), "pe")
    S.wait("vector", "pe", pe_bc)
    S.wait("vector", ss, t_ali)
    v_btab = S.inc("vector", lambda e: e.tensor_scalar(
        btab[:], alib[:], pbc[:, 0:1], None, ALU.add), "dve")

    a_dnum_prev = 0
    v_recip_prev = 0
    v_ctxt_prev = v_btab
    pe_op_last = 0
    bank_free = {0: v_last, 1: v_last}
    store_t = {}
    vadd_t = {}
    for tc in range(4):
        for h in range(16):
            cc, hh = h // 2, (h % 2) * 64
            ev_exp = {}
            pe_num = {}
            for sc in range(16):
                alt = sc % 2
                if sc >= 2:
                    S.wait("tensor", "act", ev_exp[sc - 2])
                if tc == 0 and h == 0 and sc == 0:
                    S.wait("tensor", "dve", v_last)
                    S.wait("tensor", "dve", v_btab)
                pe_st = S.inc("tensor", lambda e, alt=alt, cc=cc, hh=hh, sc=sc, tc=tc: e.matmul(
                    pst[alt][:], kt[cc][hh:hh + 64, 128 * sc:128 * (sc + 1)],
                    qt[cc][hh:hh + 64, 512 * tc:512 * (tc + 1)],
                    start=True, stop=True), "pe")
                S.wait("scalar", "pe", pe_st)
                if sc >= 2:
                    S.wait("scalar", "pe", pe_num[sc - 2])
                ev_exp[sc] = S.inc("scalar", lambda e, alt=alt, h=h, sc=sc: e.activation(
                    expb[alt][:], pst[alt][:], AF.Exp,
                    bias=btab[:, 16 * h + sc:16 * h + sc + 1], scale=0.125), "act")
                S.wait("tensor", "act", ev_exp[sc])
                if sc == 0:
                    S.wait("tensor", "act", a_dnum_prev)
                    S.wait("tensor", "dve", v_recip_prev)
                pe_num[sc] = S.inc("tensor", lambda e, alt=alt, sc=sc, h=h: e.matmul(
                    pnum[0:65, :], vt[sc][:, 65 * h:65 * h + 65], expb[alt][:],
                    start=(sc == 0), stop=(sc == 15)), "pe")
            S.wait("scalar", "pe", pe_num[15])
            S.wait("scalar", "dve", v_ctxt_prev)
            a_dnum = S.inc("scalar", lambda e: e.copy(dnum[0:64, :], pnum[0:64, :]), "act")
            S.wait("vector", "act", a_dnum)
            v_recip = S.inc("vector", lambda e: e.reciprocal(drec[:], pnum[64:65, :]), "dve")
            S.wait("tensor", "dve", v_recip)
            S.wait("tensor", "dve", v_ctxt_prev)
            pe_bc2 = S.inc("tensor", lambda e: e.matmul(
                pbc[0:64, :], ones1f[0:1, 0:64], drec[:], start=True, stop=True), "pe")
            S.wait("vector", "pe", pe_bc2)
            if h == 0 and tc > 0:
                S.wait("vector", "pe", pe_op_last)
            v_ctxt = S.inc("vector", lambda e, cc=cc, hh=hh: e.tensor_mul(
                ctxt[cc][hh:hh + 64, :], dnum[0:64, :], pbc[0:64, :]), "dve")
            a_dnum_prev, v_recip_prev, v_ctxt_prev = a_dnum, v_recip, v_ctxt
        for tsub in range(4):
            ot = tc * 4 + tsub
            i_abs = row0 + 128 * ot
            alt = tsub % 2
            if ot >= 2:
                S.wait("sync", "dve", vadd_t.get((ot - 2, 1), 0))
            r_load = S.dma("sync", sr[alt], rb2[alt][:], r_in[i_abs:i_abs + 128, :])
            for half in range(2):
                bank = (pA, pB)[half]
                if ot == 0 and half == 0:
                    S.wait("tensor", sw, wo_t)
                S.wait("tensor", "dve", bank_free[half])
                if tsub == 0:
                    S.wait("tensor", "dve", v_ctxt_prev)
                for kc in range(8):
                    S.op("tensor", lambda e, bank=bank, kc=kc, tsub=tsub, half=half: e.matmul(
                        bank[:], ctxt[kc][:, 128 * tsub:128 * (tsub + 1)],
                        wo[kc][:, 512 * half:512 * (half + 1)],
                        start=(kc == 0), stop=False))
                pe_op_last = S.inc("tensor", lambda e, bank=bank, half=half: e.matmul(
                    bank[:], ones1b[:], browB[0:1, D + 512 * half:D + 512 * (half + 1)],
                    start=False, stop=True), "pe")
                S.wait("vector", "pe", pe_op_last)
                S.wait("vector", sr[alt], r_load)
                if ot >= 2:
                    S.wait("vector", so[alt], store_t.get(ot - 2, 0))
                vadd_t[(ot, half)] = S.inc(
                    "vector", lambda e, bank=bank, alt=alt, half=half: e.tensor_add(
                        rnew[alt][:, 512 * half:512 * (half + 1)], bank[:],
                        rb2[alt][:, 512 * half:512 * (half + 1)]), "dve")
                bank_free[half] = vadd_t[(ot, half)]
            S.wait("gpsimd", "dve", vadd_t[(ot, 1)])
            store_t[ot] = S.dma("gpsimd", so[alt], r_out[i_abs:i_abs + 128, :], rnew[alt][:])


def prep_w_l1_attn_a(inputs):
    w = {}
    w["Wq"] = np.ascontiguousarray(inputs["L1_attn_a_Wq"], dtype=np.float32)
    w["Wk"] = np.ascontiguousarray(inputs["L1_attn_a_Wk"], dtype=np.float32)
    w["Wv"] = np.ascontiguousarray(inputs["L1_attn_a_Wv"], dtype=np.float32)
    w["Wo"] = np.ascontiguousarray(inputs["L1_attn_a_Wo"], dtype=np.float32)
    bq = inputs["L1_attn_a_bq"].astype(np.float32)
    bk = inputs["L1_attn_a_bk"].astype(np.float32)
    w["bqk_cols"] = np.stack([np.concatenate([bq.reshape(8, 128).T,
                                              bk.reshape(8, 128).T], axis=1)])[0]
    import ml_dtypes
    bv = inputs["L1_attn_a_bv"].astype(ml_dtypes.bfloat16)
    bo = inputs["L1_attn_a_bo"].astype(ml_dtypes.bfloat16)
    w["bvbo_row"] = np.concatenate([bv, bo]).reshape(1, 2 * D)
    return w


def build_sigma5_program():
    es = ExitStack()
    nc = bass.Bass("TRN2", target_bir_lowering=False)
    rin = nc.dram_tensor("rin", [NTOK, D], F32, kind="ExternalInput").ap()
    rout = nc.dram_tensor("rout", [NTOK, D], F32, kind="ExternalOutput").ap()
    xn = nc.dram_tensor("xn", [NTOK, D], BF16, kind="Internal").ap()
    wnames = {"Wq": [D, D], "Wk": [D, D], "Wv": [D, D], "Wo": [D, D],
              "bqk_cols": [128, 16]}
    w = {k: nc.dram_tensor("w_" + k, s, F32, kind="ExternalInput").ap()
         for k, s in wnames.items()}
    w["bvbo_row"] = nc.dram_tensor("w_bvbo_row", [1, 2 * D], BF16,
                                   kind="ExternalInput").ap()
    cst = {"alibi": nc.dram_tensor("c_alibi", [128, 256], F32,
                                   kind="ExternalInput").ap()}
    ctx = Ctx(nc, es)
    S = Script(ctx)
    plan_l1_attn_a(S, ctx, rin, rout, xn, w, cst)
    for p in ctx.pipe:
        if p.v:
            S.wait("gpsimd", p, p.v)
    S.emit()
    es.close()
    return nc


def plan_proj_pre(S, ctx, out_tiles, xnt, wtiles, w_sem, w_ready, x_sem, xnt_ready,
                  pbanks, bias_col, prev_evac=0, prev_pe_for_out=0):
    ngr = T // 512
    evac_t = {}
    pe_t = 0
    for dc in range(8):
        for g in range(ngr):
            idx = dc * ngr + g
            bank = pbanks[idx % 2]
            if idx >= 2:
                S.wait("tensor", "act", evac_t[idx - 2])
            if idx == 0:
                S.wait("tensor", w_sem, w_ready)
                S.wait("tensor", x_sem, xnt_ready)
                S.wait("tensor", "act", prev_evac)
                S.wait("scalar", "pe", prev_pe_for_out)
            for kc in range(7):
                S.op("tensor", lambda e, bank=bank, kc=kc, dc=dc, g=g: e.matmul(
                    bank[:], wtiles[kc][:, 128 * dc:128 * (dc + 1)],
                    xnt[kc][:, 512 * g:512 * (g + 1)], start=(kc == 0), stop=False))
            pe_t = S.inc("tensor", lambda e, bank=bank, dc=dc, g=g: e.matmul(
                bank[:], wtiles[7][:, 128 * dc:128 * (dc + 1)],
                xnt[7][:, 512 * g:512 * (g + 1)], start=False, stop=True), "pe")
            S.wait("scalar", "pe", pe_t)
            bc = bias_col(dc)
            evac_t[idx] = S.inc("scalar", lambda e, bank=bank, dc=dc, g=g, bc=bc: e.add(
                out_tiles[dc][:, 512 * g:512 * (g + 1)], bank[:], bc), "act")
    return pe_t, evac_t[8 * ngr - 1]


def plan_sm_proj(S, ctx, preT, wtiles, w_sem, w_ready, pbanks, bias_row,
                 sms, normalize, out_fn, prev_evac=0, post_fn=None, exs=None):
    a_exp = [0] * 16
    pe_t = 0
    v_stats_prev = 0
    for i in range(16):
        for half in range(2):
            bank = pbanks[half]
            if i == 0 and half == 0:
                S.wait("tensor", w_sem, w_ready)
                S.wait("tensor", "act", prev_evac)
            if i > 0 and half == 0:
                S.wait("tensor", a_exp[i - 1][0], a_exp[i - 1][1])
            for kc in range(8):
                S.op("tensor", lambda e, bank=bank, kc=kc, i=i, half=half: e.matmul(
                    bank[:], preT[kc][:, 128 * i:128 * (i + 1)],
                    wtiles[kc][:, 512 * half:512 * (half + 1)],
                    start=(kc == 0), stop=False))
            ob, rows, base = bias_row
            pe_t = S.inc("tensor", lambda e, bank=bank, half=half, ob=ob, rows=rows,
                         base=base: e.matmul(
                bank[:], ob[:], rows[0:1, base + 512 * half:base + 512 * (half + 1)],
                start=False, stop=True), "pe")
        S.wait("vector", "pe", pe_t)
        if i > 0:
            S.wait("vector", a_exp[i - 1][0], a_exp[i - 1][1])
        S.op("vector", lambda e: e.reduce_max(sms[:, 0:1], pbanks[0][:], AX.X))
        S.op("vector", lambda e: e.reduce_max(sms[:, 1:2], pbanks[1][:], AX.X))
        S.op("vector", lambda e: e.tensor_max(sms[:, 2:3], sms[:, 0:1], sms[:, 1:2]))
        v_stats = S.inc("vector", lambda e: e.tensor_scalar_mul(
            sms[:, 3:4], sms[:, 2:3], -1.0), "dve")
        out_tile, war = out_fn(i)
        if war is not None:
            war("scalar")
        S.wait("scalar", "dve", v_stats)
        etile = exs if normalize else out_tile
        S.op("scalar", lambda e, etile=etile: e.activation(
            etile[:, 0:512], pbanks[0][:], AF.Exp, bias=sms[:, 3:4],
            accum_out=sms[:, 4:5]))
        a_e = S.inc("scalar", lambda e, etile=etile: e.activation(
            etile[:, 512:1024], pbanks[1][:], AF.Exp, bias=sms[:, 3:4],
            accum_out=sms[:, 5:6]), "act")
        if normalize:
            S.wait("vector", "act", a_e)
            S.op("vector", lambda e: e.tensor_add(sms[:, 6:7], sms[:, 4:5], sms[:, 5:6]))
            S.op("vector", lambda e: e.reciprocal(sms[:, 7:8], sms[:, 6:7]))
            a_e = S.inc("vector", lambda e, out_tile=out_tile: e.tensor_scalar(
                out_tile[:], exs[:], sms[:, 7:8], None, ALU.mult), "dve")
            a_exp[i] = ("dve", a_e)
        else:
            a_exp[i] = ("act", a_e)
        if post_fn is not None:
            post_fn(i, a_e)
    return pe_t, a_exp


def plan_l1_attn_b(S, ctx, r_in, r_out, xn_ap, qb_ap, w, cst):
    with ExitStack() as es_ln:
        plan_ln(S, ctx.sub(es_ln), r_in, xn_ap, pipe0=0)
        S.barrier()
        S.emit()
    for b in range(S7_NB):
        with ExitStack() as es_b:
            _l1ab_batch(S, ctx.sub(es_b), r_in, r_out, xn_ap, qb_ap, w, cst, b)
            S.barrier()
            S.emit()


def _l1ab_batch(S, ctx, r_in, r_out, xn_ap, qb_ap, w, cst, b):
    row0 = b * T
    xnt = [ctx.sb("ab_xnt", [128, T], BF16) for _ in range(8)]
    preT = [ctx.sb("ab_pre", [128, T], BF16) for _ in range(8)]
    wbuf = [ctx.sb("ab_w", [128, D], BF16) for _ in range(8)]
    kS = [ctx.sb("ab_k", [128, D], BF16) for _ in range(16)]
    vt = [ctx.sb("ab_v", [128, 16 * 65], BF16) for _ in range(16)]
    qTt = [ctx.sb("ab_qt", [128, T], BF16) for _ in range(8)]
    qout = [ctx.sb("ab_qo", [128, D], BF16) for _ in range(2)]
    bqk = ctx.sb("ab_bqk", [128, 16], F32)
    brows = ctx.sb("ab_brows", [1, 3 * D], BF16)
    ones1b = ctx.sb("ab_o1b", [1, 128], BF16)
    sms = ctx.sb("ab_sms", [128, 8], F32)
    exs_t = ctx.sb("ab_exs", [128, D], BF16)
    maugS = ctx.sb("ab_maug", [128, 8 * 65], BF16)
    rden = ctx.sb("ab_rden", [128, 16], F32)
    rb2 = [ctx.sb("ab_rb", [128, D], F32) for _ in range(2)]
    rnew = [ctx.sb("ab_rn", [128, D], F32) for _ in range(2)]

    pA = ctx.ps("ab_pA", [128, 512])
    pB = ctx.ps("ab_pB", [128, 512])
    pM = [ctx.ps("ab_pM", [128, 512]) for _ in range(4)]

    sx = ctx.pipe[8]
    sw = ctx.pipe[9]
    ss = ctx.pipe[10]
    sq = [ctx.pipe[11], ctx.pipe[12]]
    sqt = ctx.pipe[13]
    sr = [ctx.pipe[14], ctx.pipe[15]]
    so = [ctx.pipe[16], ctx.pipe[17]]

    xnt_t = plan_load_xnt(S, sx, xnt, xn_ap, row0)
    wq_t = plan_load_w_chunks(S, sw, wbuf, w["WqT"])
    t_bqk = S.dma("sync", ss, bqk[:], w["bqk_cols"][:, :])
    t_brows = S.dma("sync", ss, brows[:], w["b_rows"][:, :])
    S.op("vector", lambda e: e.memset(ones1b[:], 1.0))
    for st_ in range(16):
        S.op("vector", lambda e, st_=st_: e.memset(
            vt[st_][:].rearrange("p (h c) -> p h c", c=65)[:, :, 64:65], 1.0))
    v_setup = S.inc("vector", lambda e: e.memset(sms[:, 0:1], 0.0), "dve")

    S.wait("scalar", ss, t_brows)
    S.wait("tensor", "dve", v_setup)
    S.wait("tensor", ss, t_brows)
    pe_p1, ev_p1 = plan_proj_fm(S, ctx, preT, xnt, wbuf, sw, wq_t, sx, xnt_t,
                                (pA, pB), bias_col=lambda dc: bqk[:, dc:dc + 1])
    if S7_PHASES < 2:
        return
    S.wait("gpsimd", "pe", pe_p1)
    wfq_t = plan_load_w_chunks(S, sw, wbuf, w["WfqT"])
    q_store = {}

    def q_out(i):
        sl = i % 2

        def war(eng):
            if i >= 2:
                S.wait(eng, sq[sl], q_store.get(i - 2, 0))
        return qout[sl], war

    def q_post(i, a_e):
        S.wait("gpsimd", "act", a_e)
        q_store[i] = S.dma("gpsimd", sq[i % 2],
                           qb_ap[row0 + 128 * i:row0 + 128 * (i + 1), :],
                           qout[i % 2][:])

    pe_q2, a_q2 = plan_sm_proj(S, ctx, preT, wbuf, sw, wfq_t, (pA, pB),
                               (ones1b, brows, 0), sms, False, q_out,
                               prev_evac=ev_p1, post_fn=q_post)
    if S7_PHASES < 3:
        return
    S.wait("gpsimd", "pe", pe_q2)
    wk_t = plan_load_w_chunks(S, sw, wbuf, w["WkT"])
    pe_p3, ev_p3 = plan_proj_pre(S, ctx, preT, xnt, wbuf, sw, wk_t, sx, xnt_t,
                                 (pA, pB), bias_col=lambda dc: bqk[:, 8 + dc:9 + dc],
                                 prev_evac=a_q2[15][1], prev_pe_for_out=pe_q2)
    if S7_PHASES < 4:
        return
    S.wait("gpsimd", "pe", pe_p3)
    wfk_t = plan_load_w_chunks(S, sw, wbuf, w["WfkT"])
    pe_k2, a_k2 = plan_sm_proj(S, ctx, preT, wbuf, sw, wfk_t, (pA, pB),
                               (ones1b, brows, D), sms, True,
                               lambda i: (kS[i], None), prev_evac=ev_p3,
                               exs=exs_t)
    if S7_PHASES < 5:
        return
    S.wait("gpsimd", "pe", pe_k2)
    wv_t = plan_load_w_chunks(S, sw, wbuf, w["WvT"])
    ev_v = [0] * 32
    for st_ in range(16):
        for half in range(2):
            idx = st_ * 2 + half
            bank = (pA, pB)[idx % 2]
            if idx >= 2:
                S.wait("tensor", "dve", ev_v[idx - 2])
            if idx == 0:
                S.wait("tensor", sw, wv_t)
                S.wait("tensor", "dve", v_setup)
                S.wait("tensor", a_k2[15][0], a_k2[15][1])
            for kc in range(8):
                S.op("tensor", lambda e, bank=bank, kc=kc, st_=st_, half=half: e.matmul(
                    bank[:], xnt[kc][:, 128 * st_:128 * (st_ + 1)],
                    wbuf[kc][:, 512 * half:512 * (half + 1)],
                    start=(kc == 0), stop=False))
            pe_v = S.inc("tensor", lambda e, bank=bank, half=half: e.matmul(
                bank[:], ones1b[:], brows[0:1, 2 * D + 512 * half:2 * D + 512 * (half + 1)],
                start=False, stop=True), "pe")
            S.wait("vector", "pe", pe_v)
            ev_v[idx] = S.inc("vector", lambda e, bank=bank, st_=st_, half=half: e.tensor_copy(
                vt[st_][:, half * 520:(half + 1) * 520]
                .rearrange("p (h c) -> p h c", c=65)[:, :, 0:64],
                bank[:].rearrange("p (h d) -> p h d", d=64)), "dve")
    v_last = ev_v[31]

    if S7_PHASES < 6:
        return
    S.wait("sync", sq[0], q_store[14])
    S.wait("sync", sq[1], q_store[15])
    qt_t = plan_load_xnt(S, sqt, qTt, qb_ap, row0)

    if S7_PHASES < 7:
        return
    a_maug = {}
    pe_m = {}
    for h in range(16):
        alt = h % 2
        if h >= 2:
            S.wait("tensor", "act", a_maug[h - 2])
        if h == 0:
            S.wait("tensor", "dve", v_last)
        for sc in range(16):
            fn = lambda e, alt=alt, sc=sc, h=h: e.matmul(
                pM[alt][0:64, 0:65], kS[sc][:, 64 * h:64 * (h + 1)],
                vt[sc][:, 65 * h:65 * h + 65], start=(sc == 0), stop=(sc == 15))
            if sc == 15:
                pe_m[h] = S.inc("tensor", fn, "pe")
            else:
                S.op("tensor", fn)
        S.wait("scalar", "pe", pe_m[h])
        a_maug[h] = S.inc("scalar", lambda e, h=h, hh2=(h % 2) * 64: e.copy(
            maugS[hh2:hh2 + 64, 65 * (h // 2):65 * (h // 2) + 65],
            pM[h % 2][0:64, 0:65]), "act")

    if S7_PHASES < 8:
        return
    vadd_t = {}
    store_t = {}
    v_read_prev = 0
    for i in range(16):
        i_abs = row0 + 128 * i
        alt = i % 2
        if i >= 2:
            S.wait("sync", "dve", vadd_t.get(i - 2, 0))
        r_load = S.dma("sync", sr[alt], rb2[alt][:], r_in[i_abs:i_abs + 128, :])
        S.wait("tensor", "dve", v_read_prev)
        if i == 0:
            S.wait("tensor", "act", a_maug[14])
            S.wait("tensor", "act", a_maug[15])
            S.wait("tensor", sqt, qt_t)
        pe_o = 0
        for h in range(16):
            cc, hh = h // 2, (h % 2) * 64
            bank = pM[h % 4]
            col = 66 * (h // 4)
            fn = lambda e, bank=bank, cc=cc, hh=hh, i=i, col=col: e.matmul(
                bank[0:128, col:col + 65],
                qTt[cc][hh:hh + 64, 128 * i:128 * (i + 1)],
                maugS[hh:hh + 64, 65 * cc:65 * cc + 65],
                start=True, stop=True)
            pe_o = S.inc("tensor", fn, "pe")
        S.wait("vector", "pe", pe_o)
        S.wait("vector", sr[alt], r_load)
        if i >= 2:
            S.wait("vector", so[alt], store_t.get(i - 2, 0))
        for h in range(16):
            bank = pM[h % 4]
            col = 66 * (h // 4)
            S.op("vector", lambda e, bank=bank, col=col, h=h: e.reciprocal(
                rden[:, h:h + 1], bank[:, col + 64:col + 65]))
            fn = lambda e, bank=bank, col=col, h=h, alt=alt: e.scalar_tensor_tensor(
                rnew[alt][:, 64 * h:64 * (h + 1)], bank[:, col:col + 64],
                rden[:, h:h + 1], rb2[alt][:, 64 * h:64 * (h + 1)],
                ALU.mult, ALU.add)
            if h == 15:
                vadd_t[i] = S.inc("vector", fn, "dve")
            else:
                S.op("vector", fn)
        v_read_prev = vadd_t[i]
        S.wait("gpsimd", "dve", vadd_t[i])
        store_t[i] = S.dma("gpsimd", so[alt], r_out[i_abs:i_abs + 128, :], rnew[alt][:])


def prep_w_l1_attn_b(inputs):
    import ml_dtypes
    w = {}
    w["WqT"] = np.ascontiguousarray(inputs["L1_attn_b_Wq"].T.astype(np.float32))
    w["WkT"] = np.ascontiguousarray(inputs["L1_attn_b_Wk"].T.astype(np.float32))
    w["WvT"] = np.ascontiguousarray(inputs["L1_attn_b_Wv"].T.astype(np.float32))
    w["WfqT"] = np.ascontiguousarray(inputs["L1_attn_b_Wfq"].T.astype(np.float32))
    w["WfkT"] = np.ascontiguousarray(inputs["L1_attn_b_Wfk"].T.astype(np.float32))
    bq = inputs["L1_attn_b_bq"].astype(np.float32)
    bk = inputs["L1_attn_b_bk"].astype(np.float32)
    w["bqk_cols"] = np.concatenate([bq.reshape(8, 128).T, bk.reshape(8, 128).T], axis=1)
    bfq = inputs["L1_attn_b_bfq"].astype(ml_dtypes.bfloat16)
    bfk = inputs["L1_attn_b_bfk"].astype(ml_dtypes.bfloat16)
    bv = inputs["L1_attn_b_bv"].astype(ml_dtypes.bfloat16)
    w["b_rows"] = np.concatenate([bfq, bfk, bv]).reshape(1, 3 * D)
    return w


def build_sigma7_program():
    es = ExitStack()
    nc = bass.Bass("TRN2", target_bir_lowering=False)
    rin = nc.dram_tensor("rin", [NTOK, D], F32, kind="ExternalInput").ap()
    rout = nc.dram_tensor("rout", [NTOK, D], F32, kind="ExternalOutput").ap()
    xn = nc.dram_tensor("xn", [NTOK, D], BF16, kind="Internal").ap()
    qb = nc.dram_tensor("qb", [NTOK, D], BF16, kind="Internal").ap()
    w = {}
    for k in ("WqT", "WkT", "WvT", "WfqT", "WfkT"):
        w[k] = nc.dram_tensor("w_" + k, [D, D], F32, kind="ExternalInput").ap()
    w["bqk_cols"] = nc.dram_tensor("w_bqk_cols", [128, 16], F32, kind="ExternalInput").ap()
    w["b_rows"] = nc.dram_tensor("w_b_rows", [1, 3 * D], BF16, kind="ExternalInput").ap()
    ctx = Ctx(nc, es)
    S = Script(ctx)
    plan_l1_attn_b(S, ctx, rin, rout, xn, qb, w, {})
    for p in ctx.pipe:
        if p.v:
            S.wait("gpsimd", p, p.v)
    S.emit()
    es.close()
    return nc


def kernel(**inputs):
    from concourse.bass_utils import run_bass_kernel_spmd
    x = np.asarray(inputs["L0_attn_a_x"], dtype=np.float32)
    r = np.ascontiguousarray(x.reshape(8, NTOK, D))
    cst = host_consts()
    cst.update(consts_l0_attn_a())
    cst.update(consts_l0_ffn())

    w1 = prep_w_l0_attn_a(inputs)
    w1m = {"Wq": w1["Wq"], "Wk": w1["Wk"], "Wv": w1["Wv"],
           "pkv": w1["pkv_pack"], "gn": w1["gn_cols"]}
    w2 = prep_w_l0_ffn(inputs, 0)
    w3 = prep_w_l0_attn_b(inputs)
    w5 = prep_w_l1_attn_a(inputs)
    w6 = prep_w_l1_ffn(inputs)
    w7 = prep_w_l1_attn_b(inputs)

    p1 = build_sigma1_program()
    p2 = build_sigma2_program(n_cores=8)
    p3 = build_sigma3_program()
    p5 = build_sigma5_program()
    p6 = build_sigma6_program()
    p7 = build_sigma7_program()

    c1 = ("Gq", "Gk", "Gnt", "cmask512", "gemask512", "EC", "ID2")
    c2 = ("lt128", "iota8", "c99m", "iota8col", "ID8")
    seq = [(p1, w1m, c1), (p2, w2, c2), (p3, w3, ("cmask512",)),
           (p2, w2, c2), (p5, w5, ("alibi",)), (p6, w6, ("ID128",)),
           (p7, w7, ()), (p6, w6, ("ID128",))]
    for prog, w, cs in seq:
        in_maps = []
        for c in range(8):
            im = {"rin": r[c]}
            for k, v in w.items():
                im["w_" + k] = np.ascontiguousarray(v)
            if prog is p2:
                im["w_core_id"] = np.array([[float(c)]], dtype=np.float32)
            for k in cs:
                im["c_" + k] = np.ascontiguousarray(cst[k])
            in_maps.append(im)
        res = run_bass_kernel_spmd(prog, in_maps, core_ids=list(range(8)))
        r = np.stack([np.asarray(res.results[c]["rout"]) for c in range(8)])
    return np.ascontiguousarray(r.reshape(32, 2048, D))


def plan_l0_attn_b(S, ctx, r_in, r_out, xn_ap, qb_ap, kb_ap, w, cst):
    with ExitStack() as es_ln:
        plan_ln(S, ctx.sub(es_ln), r_in, xn_ap, pipe0=0)
        S.barrier()
        S.emit()
    for b in range(B_PER_CORE):
        with ExitStack() as es_b:
            _l0ab_batch(S, ctx.sub(es_b), r_in, r_out, xn_ap, qb_ap, kb_ap, w, cst, b)
            S.barrier()
            S.emit()


def _l0ab_batch(S, ctx, r_in, r_out, xn_ap, qb_ap, kb_ap, w, cst, b):
    row0 = b * T
    xnt = [ctx.sb("ob_xnt", [128, T], BF16) for _ in range(8)]
    wbuf = [ctx.sb("ob_w", [128, D], BF16) for _ in range(8)]
    wkv = [ctx.sb("ob_wkv", [128, 128], BF16) for _ in range(8)]
    wk = ctx.sb("ob_wk", [128, 512], BF16)
    wv = ctx.sb("ob_wv", [128, 512], BF16)
    kvt = ctx.sb("ob_kvt", [128, T], BF16)
    vt = [ctx.sb("ob_v", [128, 16 * 33], BF16) for _ in range(16)]
    qTt = [ctx.sb("ob_qt", [128, T], BF16) for _ in range(8)]
    kTt = [ctx.sb("ob_kt", [128, T], BF16) for _ in range(8)]
    gr = ctx.sb("ob_gr", [1, 128], F32)
    GG = ctx.sb("ob_GG", [128, 512], F32)
    negb = ctx.sb("ob_negb", [128, 1], F32)
    ones1f = ctx.sb("ob_o1f", [1, 128], F32)
    cmask = [ctx.sb("ob_cm", [128, 512], BF16) for _ in range(4)]
    qtmp = ctx.sb("ob_qtmp", [128, D], F32)
    sq = ctx.sb("ob_sq", [128, D], F32)
    rr = ctx.sb("ob_rr", [128, 20], F32)
    q32b = [ctx.sb("ob_q32", [128, D], BF16) for _ in range(2)]
    k32b = [ctx.sb("ob_k32", [128, D], BF16) for _ in range(2)]
    expb = [ctx.sb("ob_exp", [128, 512], BF16) for _ in range(2)]
    dnum = ctx.sb("ob_dnum", [128, 512], BF16)
    drec = ctx.sb("ob_drec", [1, 512], F32)
    ctxt = [ctx.sb("ob_ctxt", [128, 512], BF16) for _ in range(8)]
    rb2 = [ctx.sb("ob_rb", [128, D], F32) for _ in range(2)]
    rnew = [ctx.sb("ob_rn", [128, D], F32) for _ in range(2)]

    pA = ctx.ps("ob_pA", [128, 512])
    pB = ctx.ps("ob_pB", [128, 512])
    pst = [ctx.ps("ob_pst", [128, 512]) for _ in range(2)]
    pst2 = [ctx.ps("ob_pst2", [128, 512]) for _ in range(2)]
    pnum = ctx.ps("ob_pnum", [128, 512])
    pbc = ctx.ps("ob_pbc", [128, 512])

    sx = ctx.pipe[8]
    sw = ctx.pipe[9]
    ss = ctx.pipe[10]
    sqs = [ctx.pipe[11], ctx.pipe[12]]
    sqt = ctx.pipe[13]
    sr = [ctx.pipe[14], ctx.pipe[15]]
    so = [ctx.pipe[16], ctx.pipe[17]]

    xnt_t = plan_load_xnt(S, sx, xnt, xn_ap, row0)
    wq_t = plan_load_w_chunks(S, sw, wbuf, w["Wq"])
    t_kv = plan_load_w_chunks(S, sw, wkv, w["Wkv"])
    t_wk = S.dma("gpsimd", sw, wk[:], w["Wk"][:, :])
    t_wv = S.dma("gpsimd", sw, wv[:], w["Wv"][:, :])
    wq_t = t_kv = t_wk = t_wv
    t_g = S.dma("sync", ss, gr[0:1, 0:96], w["g_row"][:, :])
    scm = ctx.pipe[18]
    t_cm = 0
    for j in range(4):
        t_cm = S.dma("gpsimd", scm, cmask[j][:], cst["cmask512"][128 * j:128 * (j + 1), :])
    S.op("vector", lambda e: e.memset(ones1f[:], 1.0))
    for st_ in range(16):
        S.op("vector", lambda e, st_=st_: e.memset(
            vt[st_][:].rearrange("p (h c) -> p h c", c=33)[:, :, 32:33], 1.0))
    for cc in range(8):
        S.op("vector", lambda e, cc=cc: e.memset(ctxt[cc][:], 0.0))
    for t_ in (q32b[0], q32b[1], k32b[0], k32b[1]):
        S.op("vector", lambda e, t_=t_: e.memset(t_[:], 0.0))
    v_setup = S.inc("vector", lambda e: e.memset(rr[:, 0:1], 0.0), "dve")
    S.wait("vector", ss, t_g)
    S.op("vector", lambda e: e.tensor_mul(gr[0:1, 64:96], gr[0:1, 0:32], gr[0:1, 64:96]))
    S.op("vector", lambda e: e.reduce_max(gr[0:1, 96:97], gr[0:1, 64:96], AX.X,
                                          apply_absolute_value=True))
    v_g = S.inc("vector", lambda e: e.tensor_scalar_mul(
        gr[0:1, 97:98], gr[0:1, 96:97], -8.0 * 0.125 * math.sqrt(64.0 * 32.0)), "dve")
    S.wait("tensor", "dve", v_g)
    pe_nb = S.inc("tensor", lambda e: e.matmul(pbc[:, 0:1], ones1f[:], gr[0:1, 97:98],
                                               start=True, stop=True), "pe")
    S.wait("vector", "pe", pe_nb)
    v_nb = S.inc("vector", lambda e: e.tensor_copy(negb[:], pbc[:, 0:1]), "dve")
    ghat_rep = gr[0:1, 64:96].rearrange("a (one r) -> a one r", one=1) \
        .broadcast_to([1, 16, 32])
    S.wait("tensor", "dve", v_nb)
    pe_gg = S.inc("tensor", lambda e: e.matmul(
        pbc[:, 0:512].rearrange("p (h r) -> p h r", h=16), ones1f[:], ghat_rep,
        start=True, stop=True), "pe")
    S.wait("vector", "pe", pe_gg)
    v_gg = S.inc("vector", lambda e: e.tensor_copy(GG[:], pbc[:, 0:512]), "dve")

    qb_v = qb_ap.rearrange("t (c p r) -> t c p r", c=8, p=2)
    q_store = {}
    ev_q = 0
    pe_q = 0
    for i in range(16):
        alt = i % 2
        bank2 = (pA, pB)
        if i >= 1:
            S.wait("tensor", "dve", ev_q)
        if i == 0:
            S.wait("tensor", sw, wq_t)
            S.wait("tensor", sx, xnt_t)
            S.wait("tensor", "dve", v_gg)
        for half in range(2):
            for kc in range(8):
                fn = lambda e, half=half, kc=kc, i=i: e.matmul(
                    bank2[half][:], xnt[kc][:, 128 * i:128 * (i + 1)],
                    wbuf[kc][:, 512 * half:512 * (half + 1)],
                    start=(kc == 0), stop=(kc == 7))
                if half == 1 and kc == 7:
                    pe_q = S.inc("tensor", fn, "pe")
                else:
                    S.op("tensor", fn)
        S.wait("vector", "pe", pe_q)
        S.op("vector", lambda e: e.tensor_copy(qtmp[:, 0:512], pA[:]))
        v_qc = S.inc("vector", lambda e: e.tensor_copy(qtmp[:, 512:1024], pB[:]), "dve")
        S.wait("scalar", "dve", v_qc)
        a_sq = S.inc("scalar", lambda e: e.activation(sq[:], qtmp[:], AF.Square), "act")
        S.wait("vector", "act", a_sq)
        S.op("vector", lambda e: e.tensor_reduce(
            rr[:, 0:16], sq[:].rearrange("p (h r) -> p h r", h=16), AX.X, ALU.add))
        S.op("vector", lambda e: e.tensor_scalar(
            rr[:, 0:16], rr[:, 0:16], 1.0 / 64.0, 1e-6, ALU.mult, ALU.add))
        v_rs = S.inc("vector", lambda e: e.memset(rr[:, 16:17], 0.0), "dve")
        S.wait("scalar", "dve", v_rs)
        a_rt = S.inc("scalar", lambda e: e.activation(rr[:, 0:16], rr[:, 0:16],
                                                      AF.Sqrt), "act")
        S.wait("vector", "act", a_rt)
        S.op("vector", lambda e: e.reciprocal(rr[:, 0:16], rr[:, 0:16]))
        if i >= 2:
            S.wait("vector", sqs[alt], q_store[i - 2])
        rrb = rr[:, 0:16].rearrange("p (h one) -> p h one", one=1).broadcast_to([128, 16, 32])
        ev_q = S.inc("vector", lambda e, alt=alt, rrb=rrb: e.tensor_tensor(
            q32b[alt][:].rearrange("p (c p2 r) -> p (c p2) r", c=8, p2=2)[:, :, 0:32],
            qtmp[:].rearrange("p (h r) -> p h r", h=16)[:, :, 0:32], rrb, ALU.mult),
            "dve")
        S.wait("gpsimd", "dve", ev_q)
        q_store[i] = S.dma("gpsimd", sqs[alt],
                           qb_ap[row0 + 128 * i:row0 + 128 * (i + 1), :],
                           q32b[alt][:])

    ev_kv = 0
    pe_kv = 0
    for g in range(4):
        bank = (pA, pB)[g % 2]
        if g >= 2:
            S.wait("tensor", "act", ev_kv)
        if g == 0:
            S.wait("tensor", sw, t_kv)
            S.wait("tensor", "dve", ev_q)
        for kc in range(8):
            fn = lambda e, bank=bank, kc=kc, g=g: e.matmul(
                bank[:], wkv[kc][:, 0:128],
                xnt[kc][:, 512 * g:512 * (g + 1)],
                start=(kc == 0), stop=(kc == 7))
            if kc == 7:
                pe_kv = S.inc("tensor", fn, "pe")
            else:
                S.op("tensor", fn)
        S.wait("scalar", "pe", pe_kv)
        ev_kv = S.inc("scalar", lambda e, bank=bank, g=g: e.copy(
            kvt[:, 512 * g:512 * (g + 1)], bank[:]), "act")

    kb_v = kb_ap.rearrange("t (c p r) -> t c p r", c=8, p=2)
    k_store = {}
    ev_k = 0
    ev_v = 0
    pe_kx = 0
    S.wait("gpsimd", sqs[0], q_store[14])
    S.wait("gpsimd", sqs[1], q_store[15])
    for i in range(16):
        alt = i % 2
        if i >= 1:
            S.wait("tensor", "dve", ev_v)
        if i == 0:
            S.wait("tensor", sw, t_wv)
            S.wait("tensor", "act", ev_kv)
        fnk = lambda e, i=i: e.matmul(pA[:], kvt[:, 128 * i:128 * (i + 1)], wk[:],
                                      start=True, stop=True)
        S.op("tensor", fnk)
        pe_kx = S.inc("tensor", lambda e, i=i: e.matmul(
            pB[:], kvt[:, 128 * i:128 * (i + 1)], wv[:], start=True, stop=True), "pe")
        S.wait("vector", "pe", pe_kx)
        S.op("vector", lambda e: e.tensor_copy(qtmp[:, 0:512], pA[:]))
        v_kc = S.inc("vector", lambda e: e.memset(rr[:, 17:18], 0.0), "dve")
        S.wait("scalar", "dve", v_kc)
        a_sq = S.inc("scalar", lambda e: e.activation(sq[:, 0:512], qtmp[:, 0:512],
                                                      AF.Square), "act")
        S.wait("vector", "act", a_sq)
        S.op("vector", lambda e: e.tensor_reduce(
            rr[:, 0:16], sq[:, 0:512].rearrange("p (h r) -> p h r", h=16), AX.X, ALU.add))
        S.op("vector", lambda e: e.tensor_scalar(
            rr[:, 0:16], rr[:, 0:16], 1.0 / 32.0, 1e-6, ALU.mult, ALU.add))
        v_rs = S.inc("vector", lambda e: e.memset(rr[:, 16:17], 0.0), "dve")
        S.wait("scalar", "dve", v_rs)
        a_rt = S.inc("scalar", lambda e: e.activation(rr[:, 0:16], rr[:, 0:16],
                                                      AF.Sqrt), "act")
        S.wait("vector", "act", a_rt)
        S.op("vector", lambda e: e.reciprocal(rr[:, 0:16], rr[:, 0:16]))
        rrb = rr[:, 0:16].rearrange("p (h one) -> p h one", one=1).broadcast_to([128, 16, 32])
        S.op("vector", lambda e, rrb=rrb: e.tensor_tensor(
            sq[:, 0:512].rearrange("p (h r) -> p h r", h=16),
            qtmp[:, 0:512].rearrange("p (h r) -> p h r", h=16), rrb, ALU.mult))
        if i >= 2:
            S.wait("vector", sqs[alt], k_store[i - 2])
        ev_k = S.inc("vector", lambda e, alt=alt: e.tensor_tensor(
            k32b[alt][:].rearrange("p (c p2 r) -> p (c p2) r", c=8, p2=2)[:, :, 0:32],
            sq[:, 0:512].rearrange("p (h r) -> p h r", h=16),
            GG[:].rearrange("p (h r) -> p h r", h=16), ALU.mult), "dve")
        S.wait("gpsimd", "dve", ev_k)
        k_store[i] = S.dma("gpsimd", sqs[alt],
                           kb_ap[row0 + 128 * i:row0 + 128 * (i + 1), :],
                           k32b[alt][:])
        ev_v = S.inc("vector", lambda e, i=i: e.tensor_copy(
            vt[i][:].rearrange("p (h c) -> p h c", c=33)[:, :, 0:32],
            pB[:].rearrange("p (h d) -> p h d", d=32)), "dve")

    S.wait("sync", sqs[0], k_store[14])
    S.wait("sync", sqs[1], k_store[15])
    qt_t = plan_load_xnt(S, sqt, qTt, qb_ap, row0)
    kt_t = plan_load_xnt(S, sqt, kTt, kb_ap, row0)

    S.wait("gpsimd", "pe", S.now("pe"))
    S.wait("gpsimd", sw, t_wv)
    wo_t = plan_load_w_chunks(S, sw, wbuf, w["WoPad"])
    a_dnum_prev = 0
    v_recip_prev = 0
    v_ctxt_prev = v_gg
    pe_op_last = 0
    bank_free = {0: 0, 1: 0}
    store_t = {}
    vadd_t = {}
    for tc in range(4):
        for h in range(16):
            cc, hh = h // 2, (h % 2) * 64
            psth = pst if h % 2 == 0 else pst2
            nsc = 4 * tc + 4
            ev_exp = {}
            pe_num = {}
            for sc in range(nsc):
                alt = sc % 2
                if sc >= 2:
                    S.wait("tensor", ev_exp[sc - 2][0], ev_exp[sc - 2][1])
                if tc == 0 and h == 0 and sc == 0:
                    S.wait("tensor", "dve", ev_v)
                    S.wait("tensor", sqt, kt_t)
                    S.wait("tensor", "pe", pe_nb)
                    S.wait("vector", scm, t_cm)
                pe_st = S.inc("tensor", lambda e, psth=psth, alt=alt, cc=cc, hh=hh,
                              sc=sc, tc=tc: e.matmul(
                    psth[alt][:], kTt[cc][hh:hh + 32, 128 * sc:128 * (sc + 1)],
                    qTt[cc][hh:hh + 32, 512 * tc:512 * (tc + 1)],
                    start=True, stop=True), "pe")
                S.wait("scalar", "pe", pe_st)
                if sc >= 2:
                    S.wait("scalar", "pe", pe_num[sc - 2])
                a_e = S.inc("scalar", lambda e, psth=psth, alt=alt: e.activation(
                    expb[alt][:], psth[alt][:], AF.Exp, bias=negb[:, 0:1],
                    scale=0.125), "act")
                if sc >= 4 * tc:
                    j = sc - 4 * tc
                    S.wait("vector", "act", a_e)
                    a_e = S.inc("vector", lambda e, alt=alt, j=j: e.tensor_mul(
                        expb[alt][:], expb[alt][:], cmask[j][:]), "dve")
                    ev_exp[sc] = ("dve", a_e)
                    S.wait("tensor", "dve", a_e)
                else:
                    ev_exp[sc] = ("act", a_e)
                    S.wait("tensor", "act", a_e)
                if sc == 0:
                    S.wait("tensor", "act", a_dnum_prev)
                    S.wait("tensor", "dve", v_recip_prev)
                pe_num[sc] = S.inc("tensor", lambda e, alt=alt, sc=sc, h=h,
                                   nsc=nsc: e.matmul(
                    pnum[0:33, :], vt[sc][:, 33 * h:33 * h + 33], expb[alt][:],
                    start=(sc == 0), stop=(sc == nsc - 1)), "pe")
            S.wait("scalar", "pe", pe_num[nsc - 1])
            S.wait("scalar", "dve", v_ctxt_prev)
            a_dnum = S.inc("scalar", lambda e: e.copy(dnum[0:32, :], pnum[0:32, :]), "act")
            S.wait("vector", "act", a_dnum)
            v_recip = S.inc("vector", lambda e: e.reciprocal(drec[:], pnum[32:33, :]), "dve")
            S.wait("tensor", "dve", v_recip)
            S.wait("tensor", "dve", v_ctxt_prev)
            pe_bc2 = S.inc("tensor", lambda e: e.matmul(
                pbc[0:32, :], ones1f[0:1, 0:32], drec[:], start=True, stop=True), "pe")
            S.wait("vector", "pe", pe_bc2)
            if h == 0 and tc > 0:
                S.wait("vector", "pe", pe_op_last)
            v_ctxt = S.inc("vector", lambda e, cc=cc, hh=hh: e.tensor_mul(
                ctxt[cc][hh:hh + 32, :], dnum[0:32, :], pbc[0:32, :]), "dve")
            a_dnum_prev, v_recip_prev, v_ctxt_prev = a_dnum, v_recip, v_ctxt
        for tsub in range(4):
            ot = tc * 4 + tsub
            i_abs = row0 + 128 * ot
            alt = tsub % 2
            if ot >= 2:
                S.wait("sync", "dve", vadd_t.get((ot - 2, 1), 0))
            r_load = S.dma("sync", sr[alt], rb2[alt][:], r_in[i_abs:i_abs + 128, :])
            for half in range(2):
                bank = (pA, pB)[half]
                if ot == 0 and half == 0:
                    S.wait("tensor", sw, wo_t)
                S.wait("tensor", "dve", bank_free[half])
                if tsub == 0:
                    S.wait("tensor", "dve", v_ctxt_prev)
                for kc in range(8):
                    fn = lambda e, bank=bank, kc=kc, tsub=tsub, half=half: e.matmul(
                        bank[:], ctxt[kc][:, 128 * tsub:128 * (tsub + 1)],
                        wbuf[kc][:, 512 * half:512 * (half + 1)],
                        start=(kc == 0), stop=(kc == 7))
                    if kc == 7:
                        pe_op_last = S.inc("tensor", fn, "pe")
                    else:
                        S.op("tensor", fn)
                S.wait("vector", "pe", pe_op_last)
                S.wait("vector", sr[alt], r_load)
                if ot >= 2:
                    S.wait("vector", so[alt], store_t.get(ot - 2, 0))
                vadd_t[(ot, half)] = S.inc(
                    "vector", lambda e, bank=bank, alt=alt, half=half: e.tensor_add(
                        rnew[alt][:, 512 * half:512 * (half + 1)], bank[:],
                        rb2[alt][:, 512 * half:512 * (half + 1)]), "dve")
                bank_free[half] = vadd_t[(ot, half)]
            S.wait("gpsimd", "dve", vadd_t[(ot, 1)])
            store_t[ot] = S.dma("gpsimd", so[alt], r_out[i_abs:i_abs + 128, :], rnew[alt][:])


def prep_w_l0_attn_b(inputs):
    w = {}
    w["Wq"] = np.ascontiguousarray(inputs["L0_attn_b_Wq"], dtype=np.float32)
    w["Wkv"] = np.ascontiguousarray(inputs["L0_attn_b_Wkv"], dtype=np.float32)
    w["Wk"] = np.ascontiguousarray(inputs["L0_attn_b_Wk"], dtype=np.float32)
    w["Wv"] = np.ascontiguousarray(inputs["L0_attn_b_Wv"], dtype=np.float32)
    Wo = np.asarray(inputs["L0_attn_b_Wo"], dtype=np.float32)
    wop = np.zeros((D, D), dtype=np.float32)
    for h in range(16):
        wop[64 * h:64 * h + 32, :] = Wo[64 * h:64 * h + 32, :]
    w["WoPad"] = wop
    g = np.zeros((1, 96), dtype=np.float32)
    g[0, 0:64] = inputs["L0_attn_b_q_g"].astype(np.float32)
    g[0, 64:96] = inputs["L0_attn_b_k_g"].astype(np.float32)
    w["g_row"] = g
    return w


def build_sigma3_program():
    es = ExitStack()
    nc = bass.Bass("TRN2", target_bir_lowering=False)
    rin = nc.dram_tensor("rin", [NTOK, D], F32, kind="ExternalInput").ap()
    rout = nc.dram_tensor("rout", [NTOK, D], F32, kind="ExternalOutput").ap()
    xn = nc.dram_tensor("xn", [NTOK, D], BF16, kind="Internal").ap()
    qb = nc.dram_tensor("qb", [NTOK, D], BF16, kind="Internal").ap()
    kb = nc.dram_tensor("kb", [NTOK, D], BF16, kind="Internal").ap()
    w = {}
    w["Wq"] = nc.dram_tensor("w_Wq", [D, D], F32, kind="ExternalInput").ap()
    w["Wkv"] = nc.dram_tensor("w_Wkv", [D, 128], F32, kind="ExternalInput").ap()
    w["Wk"] = nc.dram_tensor("w_Wk", [128, 512], F32, kind="ExternalInput").ap()
    w["Wv"] = nc.dram_tensor("w_Wv", [128, 512], F32, kind="ExternalInput").ap()
    w["WoPad"] = nc.dram_tensor("w_WoPad", [D, D], F32, kind="ExternalInput").ap()
    w["g_row"] = nc.dram_tensor("w_g_row", [1, 96], F32, kind="ExternalInput").ap()
    cst = {"cmask512": nc.dram_tensor("c_cmask512", [512, 512], F32,
                                      kind="ExternalInput").ap()}
    ctx = Ctx(nc, es)
    S = Script(ctx)
    plan_l0_attn_b(S, ctx, rin, rout, xn, qb, kb, w, cst)
    for p in ctx.pipe:
        if p.v:
            S.wait("gpsimd", p, p.v)
    S.emit()
    es.close()
    return nc


def plan_l0_attn_a(S, ctx, r_in, r_out, xn_ap, ab_ap, w, cst):
    with ExitStack() as es_ln:
        plan_ln(S, ctx.sub(es_ln), r_in, xn_ap, pipe0=0)
        S.barrier()
        S.emit()
    for b in range(S7_NB):
        with ExitStack() as es_b:
            _l0aa_batch(S, ctx.sub(es_b), r_in, r_out, xn_ap, ab_ap, w, cst, b)
            S.barrier()
            S.emit()


def _l0aa_batch(S, ctx, r_in, r_out, xn_ap, ab_ap, w, cst, b):
    row0 = b * T
    GAM = [1.0 - 2.0 ** (-5.0 - h) for h in range(16)]
    qt = [ctx.sb("ra_qt", [128, T], BF16) for _ in range(8)]
    kt = [ctx.sb("ra_kt", [128, T], BF16) for _ in range(8)]
    vt = [ctx.sb("ra_v", [128, D], BF16) for _ in range(16)]
    ppS = [ctx.sb("ra_pp", [128, D], BF16) for _ in range(16)]
    gnt = [ctx.sb("ra_gnt", [128, 512], BF16) for _ in range(8)]
    cmask = [ctx.sb("ra_cm", [128, 512], BF16) for _ in range(4)]
    gemask = [ctx.sb("ra_gm", [128, 512], BF16) for _ in range(4)]
    onesT = ctx.sb("ra_ones", [128, 512], BF16)
    ones1f = ctx.sb("ra_o1f", [128, 1], F32)
    onesK1 = ctx.sb("ra_oK1", [1, 128], F32)
    pkvs = ctx.sb("ra_pkv", [128, 512], BF16)
    gnw = ctx.sb("ra_gnw", [1, 64], F32)
    id2 = ctx.sb("ra_id2", [2, 2], F32)
    stat = ctx.sb("ra_stat", [64, 4], F32)
    sca = ctx.sb("ra_sca", [16, 8], F32)
    t1b = ctx.sb("ra_t1", [128, 512], F32)
    stb = [ctx.sb("ra_stb", [128, 512], BF16) for _ in range(2)]
    stg = [ctx.sb("ra_stg", [64, 512], BF16) for _ in range(2)]
    sq2 = ctx.sb("ra_sq2", [64, 512], BF16)

    pA = ctx.ps("ra_pA", [128, 512])
    pB = ctx.ps("ra_pB", [128, 512])
    pst = ctx.ps("ra_pst", [128, 512])
    pst2 = ctx.ps("ra_pst2", [128, 512])
    retI = [ctx.ps("ra_retI", [128, 512]) for _ in range(2)]
    retC = [ctx.ps("ra_retC", [128, 512]) for _ in range(2)]

    sx = ctx.pipe[8]
    sw = ctx.pipe[9]
    ss = ctx.pipe[10]
    sgc = ctx.pipe[18]
    sgq = [ctx.pipe[19], ctx.pipe[20]]
    sab = [ctx.pipe[11], ctx.pipe[12]]
    sqt = [ctx.pipe[13], ctx.pipe[21]]
    sr = [ctx.pipe[14], ctx.pipe[15]]
    so = [ctx.pipe[16], ctx.pipe[17]]

    with ExitStack() as es_a:
        sub = ctx.sub(es_a)
        xnt = [sub.sb("ra_xnt", [128, T], BF16) for _ in range(8)]
        wbuf = [sub.sb("ra_w", [128, D], BF16) for _ in range(8)]
        gqb = [sub.sb("ra_gq", [128, 512], BF16) for _ in range(2)]

        xnt_t = plan_load_xnt(S, sx, xnt, xn_ap, row0)
        wq_t = plan_load_w_chunks(S, sw, wbuf, w["Wq"])
        for c in range(8):
            t_gc = S.dma("gpsimd", sgc, gnt[c][:], cst["Gnt"][128 * c:128 * (c + 1), :])
        for j in range(4):
            S.dma("gpsimd", sgc, cmask[j][:], cst["cmask512"][128 * j:128 * (j + 1), :])
            t_gc = S.dma("gpsimd", sgc, gemask[j][:],
                         cst["gemask512"][128 * j:128 * (j + 1), :])
        t_pk = S.dma("gpsimd", sgc, pkvs[:], w["pkv_pack"][:, :])
        t_gc = t_pk
        t_gn = S.dma("sync", ss, gnw[0:1, 0:64], w["gn_cols"][:, :])
        t_id = S.dma("sync", ss, id2[:], cst["ID2"][:, :])
        t_gn = t_id
        S.op("vector", lambda e: e.memset(onesT[:], 1.0))
        S.op("vector", lambda e: e.memset(stat[:], 0.0))
        S.op("vector", lambda e: e.memset(onesK1[:], 1.0))
        v_setup = S.inc("vector", lambda e: e.memset(ones1f[:], 1.0), "dve")

        ev_proj = 0
        pe_t = 0
        g_load = {}
        g_free = {0: 0, 1: 0}
        for pi, (gkey, out_tiles) in enumerate((("Gq", qt), ("Gk", kt))):
            if pi == 1:
                S.wait("gpsimd", "pe", pe_t)
                plan_load_w_chunks(S, sw, wbuf, w["Wk"])
            w_t = S.now(sw)
            for dc in range(8):
                galt = dc % 2
                S.wait("gpsimd", "dve", g_free[galt])
                g_load[(pi, dc)] = (S.dma("gpsimd", sgq[galt], gqb[galt][:],
                                          cst[gkey][128 * dc:128 * (dc + 1), :]), 0)
                for g in range(4):
                    idx = dc * 4 + g
                    bank = (pA, pB)[idx % 2]
                    if idx >= 2:
                        S.wait("tensor", "dve", ev_proj)
                    if idx == 0:
                        S.wait("tensor", sw, w_t)
                        S.wait("tensor", sx, xnt_t)
                        S.wait("tensor", "dve", v_setup)
                        S.wait("tensor", "dve", ev_proj)
                    for kc in range(8):
                        fn = lambda e, bank=bank, kc=kc, dc=dc, g=g: e.matmul(
                            bank[:], wbuf[kc][:, 128 * dc:128 * (dc + 1)],
                            xnt[kc][:, 512 * g:512 * (g + 1)],
                            start=(kc == 0), stop=(kc == 7))
                        if kc == 7:
                            pe_t = S.inc("tensor", fn, "pe")
                        else:
                            S.op("tensor", fn)
                    S.wait("vector", "pe", pe_t)
                    S.wait("vector", sgq[galt], g_load[(pi, dc)][0])
                    ev_proj = S.inc("vector", lambda e, bank=bank, dc=dc, g=g,
                                    ot=out_tiles, galt=galt: e.tensor_mul(
                        ot[dc][:, 512 * g:512 * (g + 1)], bank[:], gqb[galt][:]), "dve")
                g_free[galt] = ev_proj

        if S1_PHASES < 3:
            return
        S.wait("gpsimd", "pe", pe_t)
        wv_t = plan_load_w_chunks(S, sw, wbuf, w["Wv"])
        ev_v = 0
        for st_ in range(16):
            for half in range(2):
                idx = st_ * 2 + half
                bank = (pA, pB)[idx % 2]
                if idx >= 2:
                    S.wait("tensor", "dve", ev_v)
                if idx == 0:
                    S.wait("tensor", sw, wv_t)
                    S.wait("tensor", "dve", ev_proj)
                for kc in range(8):
                    fn = lambda e, bank=bank, kc=kc, st_=st_, half=half: e.matmul(
                        bank[:], xnt[kc][:, 128 * st_:128 * (st_ + 1)],
                        wbuf[kc][:, 512 * half:512 * (half + 1)],
                        start=(kc == 0), stop=(kc == 7))
                    if kc == 7:
                        pe_v = S.inc("tensor", fn, "pe")
                    else:
                        S.op("tensor", fn)
                S.wait("vector", "pe", pe_v)
                ev_v = S.inc("vector", lambda e, bank=bank, st_=st_, half=half:
                             e.tensor_copy(vt[st_][:, 512 * half:512 * (half + 1)],
                                           bank[:]), "dve")

        if S1_PHASES < 4:
            return
        a_pp = 0
        a_pp_prev = {0: 0, 1: 0}
        for i in range(16):
            for h in range(16):
                cc, hh = h // 2, (h % 2) * 64
                bank = (pst, pst2)[h % 2]
                S.wait("tensor", "act", a_pp_prev[h % 2])
                if i == 0 and h == 0:
                    S.wait("tensor", sgc, t_pk)
                    S.wait("tensor", "dve", ev_proj)
                pe_pp = S.inc("tensor", lambda e, cc=cc, hh=hh, i=i, h=h, bank=bank:
                              e.matmul(bank[0:128, 0:64],
                                       qt[cc][hh:hh + 64, 128 * i:128 * (i + 1)],
                                       pkvs[hh:hh + 64, 64 * (h // 2):64 * (h // 2) + 64],
                                       start=True, stop=True), "pe")
                S.wait("scalar", "pe", pe_pp)
                a_pp = S.inc("scalar", lambda e, i=i, h=h, bank=bank: e.mul(
                    ppS[i][:, 64 * h:64 * (h + 1)], bank[0:128, 0:64],
                    GAM[h] ** (512.0 * (i // 4))), "act")
                a_pp_prev[h % 2] = a_pp

    if S1_PHASES < 5:
        return
    v_merge = {}
    stb_reader = {0: 0, 1: 0}
    pst_user = {}
    stg_store = {}
    stg_stats = {}
    for tc in range(4):
        for h in range(16):
            cc, hh = h // 2, (h % 2) * 64
            psth = (pst, pst2)[h % 2]
            rI = retI[h % 2]
            rC = retC[h % 2]
            key = (tc, h)
            prevm = v_merge.get((tc, h - 2), v_merge.get((tc - 1, 14 + h % 2), 0))
            first_inner = True
            first_cross = True
            pe_in = 0
            pe_cr = 0
            for sc in range(4 * tc + 4):
                if (tc, h) == (0, 0) and sc == 0:
                    S.wait("tensor", "dve", ev_v)
                    S.wait("tensor", "act", a_pp)
                pst_prev = pst_user.get(h % 2)
                if pst_prev is not None:
                    S.wait("tensor", pst_prev[0], pst_prev[1])
                pe_st = S.inc("tensor", lambda e, psth=psth, cc=cc, hh=hh, sc=sc,
                              tc=tc: e.matmul(
                    psth[:], kt[cc][hh:hh + 64, 128 * sc:128 * (sc + 1)],
                    qt[cc][hh:hh + 64, 512 * tc:512 * (tc + 1)],
                    start=True, stop=True), "pe")
                alt = sc % 2
                diag = sc >= 4 * tc
                if diag:
                    j = sc - 4 * tc
                    S.wait("vector", "pe", pe_st)
                    S.wait("vector", "pe", stb_reader[alt])
                    if (tc, h) == (0, 0):
                        S.wait("vector", sgc, t_gc)
                    a_sc = S.inc("vector", lambda e, psth=psth, alt=alt, j=j: e.tensor_mul(
                        stb[alt][:], psth[:], cmask[j][:]), "dve")
                    pst_user[h % 2] = ("dve", a_sc)
                    S.wait("tensor", "dve", a_sc)
                else:
                    S.wait("scalar", "pe", pe_st)
                    S.wait("scalar", "pe", stb_reader[alt])
                    a_sc = S.inc("scalar", lambda e, psth=psth, alt=alt, h=h, tc=tc,
                                 sc=sc: e.mul(stb[alt][:], psth[:],
                                              GAM[h] ** (512.0 * (tc - sc // 4))),
                                 "act")
                    pst_user[h % 2] = ("act", a_sc)
                    S.wait("tensor", "act", a_sc)
                if first_inner:
                    S.wait("tensor", "dve", prevm)
                pe_in = S.inc("tensor", lambda e, rI=rI, sc=sc, h=h, alt=alt,
                              first=first_inner, last=(sc == 4 * tc + 3): e.matmul(
                    rI[0:64, :], vt[sc][:, 64 * h:64 * (h + 1)], stb[alt][:],
                    start=first, stop=last), "pe")
                stb_reader[alt] = pe_in
                first_inner = False
            for sc in range(4 * tc, 16):
                if first_cross:
                    S.wait("tensor", "dve", prevm)
                rhs = gemask[sc - 4 * tc] if sc < 4 * tc + 4 else onesT
                pe_cr = S.inc("tensor", lambda e, rC=rC, sc=sc, h=h, rhs=rhs,
                              first=first_cross, last=(sc == 15): e.matmul(
                    rC[0:64, :], ppS[sc][:, 64 * h:64 * (h + 1)], rhs[:],
                    start=first, stop=last), "pe")
                first_cross = False
            salt = (tc * 16 + h) % 2
            S.wait("vector", "pe", pe_in)
            S.wait("vector", "pe", pe_cr)
            prev_st = stg_store.get((tc * 16 + h) - 2, 0)
            S.wait("vector", sab[salt], prev_st)
            prev_sq = stg_stats.get((tc * 16 + h) - 2, 0)
            S.wait("vector", "act", prev_sq)
            S.op("vector", lambda e, rC=rC, h=h: e.tensor_mul(
                t1b[0:64, :], rC[0:64, :],
                gnt[h // 2][(h % 2) * 64:(h % 2) * 64 + 64, :]))
            v_m = S.inc("vector", lambda e, rI=rI, h=h, tc=tc, salt=salt:
                        e.scalar_tensor_tensor(
                stg[salt][:], t1b[0:64, :], GAM[h] ** (-512.0 * tc), rI[0:64, :],
                ALU.mult, ALU.add), "dve")
            v_merge[key] = v_m
            S.op("vector", lambda e, salt=salt: e.tensor_reduce(
                stat[0:64, 2:3], stg[salt][:], AX.X, ALU.add))
            v_merge[key] = S.inc("vector", lambda e: e.tensor_add(
                stat[0:64, 0:1], stat[0:64, 0:1], stat[0:64, 2:3]), "dve")
            S.wait("scalar", "dve", v_merge[key])
            a_sq = S.inc("scalar", lambda e, salt=salt: e.activation(
                sq2[:], stg[salt][:], AF.Square, accum_out=stat[0:64, 3:4]), "act")
            stg_stats[tc * 16 + h] = a_sq
            S.wait("vector", "act", a_sq)
            v_merge[key] = S.inc("vector", lambda e: e.tensor_add(
                stat[0:64, 1:2], stat[0:64, 1:2], stat[0:64, 3:4]), "dve")
            S.wait("gpsimd", "dve", v_merge[key])
            stg_store[tc * 16 + h] = S.dma(
                "gpsimd", sab[salt],
                ab_ap[128 * cc + hh:128 * cc + hh + 64, 512 * tc:512 * (tc + 1)],
                stg[salt][:])

    if S1_PHASES < 6:
        return
    with ExitStack() as es_c:
        sub = ctx.sub(es_c)
        SCB = sub.sb("ra_scb", [128, D], F32)
        BCB = sub.sb("ra_bcb", [128, D], F32)
        rb2 = [sub.sb("ra_rb", [128, D], F32) for _ in range(2)]
        rnew = [sub.sb("ra_rn", [128, D], F32) for _ in range(2)]
        rtt = [sub.sb("ra_rtt", [128, D], BF16) for _ in range(2)]
        last = v_merge[(3, 15)]
        S.wait("tensor", "dve", last)
        pu0, pu1 = pst_user.get(0), pst_user.get(1)
        if pu0:
            S.wait("tensor", pu0[0], pu0[1])
        if pu1:
            S.wait("tensor", pu1[0], pu1[1])
        pe_s = S.inc("tensor", lambda e: e.matmul(pst[0:2, 0:1], stat[0:64, 0:2],
                                                  ones1f[0:64, 0:1], start=True,
                                                  stop=True), "pe")
        S.wait("vector", "pe", pe_s)
        v_s2 = S.inc("vector", lambda e: e.tensor_copy(sca[0:2, 0:1], pst[0:2, 0:1]),
                     "dve")
        S.wait("tensor", "dve", v_s2)
        S.wait("tensor", ss, t_id)
        pe_tr = S.inc("tensor", lambda e: e.matmul(pst[0:1, 2:4], sca[0:2, 0:1],
                                                   id2[:], start=True, stop=True), "pe")
        S.wait("vector", "pe", pe_tr)
        NEL = float(16 * 64 * T)
        S.op("vector", lambda e: e.tensor_scalar_mul(sca[0:1, 2:4], pst[0:1, 2:4],
                                                     1.0 / NEL))
        S.op("vector", lambda e: e.tensor_mul(sca[0:1, 4:5], sca[0:1, 2:3],
                                              sca[0:1, 2:3]))
        S.op("vector", lambda e: e.tensor_sub(sca[0:1, 5:6], sca[0:1, 3:4],
                                              sca[0:1, 4:5]))
        v_var = S.inc("vector", lambda e: e.tensor_scalar_add(sca[0:1, 5:6],
                                                              sca[0:1, 5:6], 1e-5),
                      "dve")
        S.wait("scalar", "dve", v_var)
        a_sd = S.inc("scalar", lambda e: e.activation(sca[0:1, 6:7], sca[0:1, 5:6],
                                                      AF.Sqrt), "act")
        S.wait("vector", "act", a_sd)
        S.op("vector", lambda e: e.reciprocal(sca[0:1, 7:8], sca[0:1, 6:7]))
        S.op("vector", lambda e: e.tensor_scalar(gnw[0:1, 32:48], gnw[0:1, 0:16],
                                                 sca[0:1, 7:8], None, ALU.mult))
        S.op("vector", lambda e: e.tensor_scalar(gnw[0:1, 48:64], gnw[0:1, 32:48],
                                                 sca[0:1, 2:3], None, ALU.mult))
        v_sb = S.inc("vector", lambda e: e.tensor_sub(gnw[0:1, 48:64],
                                                      gnw[0:1, 16:32],
                                                      gnw[0:1, 48:64]), "dve")
        S.wait("tensor", "dve", v_sb)
        pe_bc = 0
        v_scb = 0
        for half in range(2):
            sview = gnw[0:1, 32 + 8 * half:40 + 8 * half] \
                .rearrange("a (h one) -> a h one", one=1).broadcast_to([1, 8, 64])
            bview = gnw[0:1, 48 + 8 * half:56 + 8 * half] \
                .rearrange("a (h one) -> a h one", one=1).broadcast_to([1, 8, 64])
            S.wait("tensor", "dve", v_scb)
            pe_bc = S.inc("tensor", lambda e, sview=sview: e.matmul(
                pst[:, 0:512].rearrange("p (h r) -> p h r", h=8),
                onesK1[:], sview, start=True, stop=True), "pe")
            S.wait("vector", "pe", pe_bc)
            v_scb = S.inc("vector", lambda e, half=half: e.tensor_copy(
                SCB[:, 512 * half:512 * (half + 1)], pst[:, 0:512]), "dve")
            S.wait("tensor", "dve", v_scb)
            pe_bc = S.inc("tensor", lambda e, bview=bview: e.matmul(
                pst2[:, 0:512].rearrange("p (h r) -> p h r", h=8),
                onesK1[:], bview, start=True, stop=True), "pe")
            S.wait("vector", "pe", pe_bc)
            v_bcb = S.inc("vector", lambda e, half=half: e.tensor_copy(
                BCB[:, 512 * half:512 * (half + 1)], pst2[:, 0:512]), "dve")
            v_scb = v_bcb

        if S1_PHASES < 7:
            return
        S.wait("sync", sab[0], stg_store[62])
        S.wait("sync", sab[1], stg_store[63])
        vadd_t = {}
        store_t = {}
        for i in range(16):
            i_abs = row0 + 128 * i
            alt = i % 2
            if i >= 2:
                S.wait("sync", "dve", vadd_t[i - 2])
            tt_t = S.dma("sync", sqt[alt], rtt[alt][:],
                         ab_ap[0:D, 128 * i:128 * (i + 1)], transpose=True)
            r_load = S.dma("sync", sr[alt], rb2[alt][:], r_in[i_abs:i_abs + 128, :])
            S.wait("vector", sqt[alt], tt_t)
            S.wait("vector", sr[alt], r_load)
            S.wait("vector", "dve", v_bcb)
            if i >= 2:
                S.wait("vector", so[alt], store_t[i - 2])
            S.op("vector", lambda e, alt=alt: e.tensor_mul(rnew[alt][:], rtt[alt][:],
                                                           SCB[:]))
            S.op("vector", lambda e, alt=alt: e.tensor_add(rnew[alt][:], rnew[alt][:],
                                                           BCB[:]))
            vadd_t[i] = S.inc("vector", lambda e, alt=alt: e.tensor_add(
                rnew[alt][:], rnew[alt][:], rb2[alt][:]), "dve")
            S.wait("gpsimd", "dve", vadd_t[i])
            store_t[i] = S.dma("gpsimd", so[alt], r_out[i_abs:i_abs + 128, :],
                               rnew[alt][:])


def prep_w_l0_attn_a(inputs):
    w = {}
    w["Wq"] = np.ascontiguousarray(inputs["L0_attn_a_Wq"], dtype=np.float32)
    w["Wk"] = np.ascontiguousarray(inputs["L0_attn_a_Wk"], dtype=np.float32)
    w["Wv"] = np.ascontiguousarray(inputs["L0_attn_a_Wv"], dtype=np.float32)
    pkv = np.asarray(inputs["L0_attn_a_past_kv"], dtype=np.float32)
    pk = np.zeros((128, 512), dtype=np.float32)
    for h in range(16):
        pk[(h % 2) * 64:(h % 2) * 64 + 64, 64 * (h // 2):64 * (h // 2) + 64] = pkv[h]
    w["pkv_pack"] = pk
    g = np.zeros((1, 64), dtype=np.float32)
    g[0, 0:16] = inputs["L0_attn_a_gn_w"].astype(np.float32)
    g[0, 16:32] = inputs["L0_attn_a_gn_b"].astype(np.float32)
    w["gn_cols"] = g
    return w


def consts_l0_attn_a():
    GAM = np.array([1.0 - 2.0 ** (-5.0 - h) for h in range(16)], dtype=np.float64)
    c = {}
    t = np.arange(512, dtype=np.float64)
    gq = np.zeros((1024, 512), dtype=np.float64)
    gk = np.zeros((1024, 512), dtype=np.float64)
    gnt = np.zeros((1024, 512), dtype=np.float64)
    for p in range(1024):
        h = 2 * (p // 128) + (p % 128) // 64
        gq[p] = GAM[h] ** t
        gk[p] = GAM[h] ** (-t)
        gnt[p] = GAM[h] ** (-t)
    c["Gq"] = gq.astype(np.float32)
    c["Gk"] = gk.astype(np.float32)
    c["Gnt"] = gnt.astype(np.float32)
    ge = np.zeros((4, 128, 512), dtype=np.float32)
    for j in range(4):
        s = 128 * j + np.arange(128)[:, None]
        tt = np.arange(512)[None, :]
        ge[j] = (s >= tt).astype(np.float32)
    c["gemask512"] = ge.reshape(512, 512)
    ec = np.zeros((16, 1024), dtype=np.float32)
    for cc in range(8):
        for p in range(128):
            ec[2 * cc + p // 64, 128 * cc + p] = 1.0
    c["EC"] = ec
    c["ID2"] = np.eye(2, dtype=np.float32)
    return c


def build_sigma1_program():
    es = ExitStack()
    nc = bass.Bass("TRN2", target_bir_lowering=False)
    rin = nc.dram_tensor("rin", [NTOK, D], F32, kind="ExternalInput").ap()
    rout = nc.dram_tensor("rout", [NTOK, D], F32, kind="ExternalOutput").ap()
    xn = nc.dram_tensor("xn", [NTOK, D], BF16, kind="Internal").ap()
    ab = nc.dram_tensor("abuf", [D, T], BF16, kind="Internal").ap()
    w = {}
    for k in ("Wq", "Wk", "Wv"):
        w[k] = nc.dram_tensor("w_" + k, [D, D], F32, kind="ExternalInput").ap()
    w["pkv_pack"] = nc.dram_tensor("w_pkv", [128, 512], F32, kind="ExternalInput").ap()
    w["gn_cols"] = nc.dram_tensor("w_gn", [1, 64], F32, kind="ExternalInput").ap()
    cst = {}
    for k, shp in (("Gq", [D, 512]), ("Gk", [D, 512]), ("Gnt", [D, 512]),
                   ("cmask512", [512, 512]), ("gemask512", [512, 512]),
                   ("EC", [16, 1024]), ("ID2", [2, 2])):
        cst[k] = nc.dram_tensor("c_" + k, shp, F32, kind="ExternalInput").ap()
    ctx = Ctx(nc, es)
    S = Script(ctx)
    plan_l0_attn_a(S, ctx, rin, rout, xn, ab, w, cst)
    for p in ctx.pipe:
        if p.v:
            S.wait("gpsimd", p, p.v)
    S.emit()
    es.close()
    return nc


def plan_l1_ffn(S, ctx, r_in, r_out, xn_ap, ct_ap, w, cst):
    with ExitStack() as es_ln:
        plan_ln(S, ctx.sub(es_ln), r_in, xn_ap, pipe0=0)
        S.barrier()
        S.emit()
    for grp in range(8):
        with ExitStack() as es_g:
            _l1ffn_group(S, ctx.sub(es_g), r_in, r_out, xn_ap, ct_ap, w, cst, grp)
            S.barrier()
            S.emit()


def _l1ffn_group(S, ctx, r_in, r_out, xn_ap, ct_ap, w, cst, grp):
    GT = 1024
    row0 = grp * GT
    NT = GT // 128
    xnt = [ctx.sb("mf_xnt", [128, GT], BF16) for _ in range(8)]
    wgb = [ctx.sb("mf_wg", [128, 512], BF16) for _ in range(8)]
    wub = [ctx.sb("mf_wu", [128, 512], BF16) for _ in range(8)]
    wdb = [ctx.sb("mf_wd", [128, D], BF16) for _ in range(16)]
    gateT = ctx.sb("mf_gateT", [128, 256], BF16)
    gbias = ctx.sb("mf_gbias", [128, 32], F32)
    idf = ctx.sb("mf_id", [128, 128], F32)
    ones1f = ctx.sb("mf_o1f", [1, 128], F32)
    sct = ctx.sb("mf_sct", [128, 80], F32)
    comb = ctx.sb("mf_comb", [128, 32], F32)
    ctS = ctx.sb("mf_ctS", [32, 128], F32)
    crow = [ctx.sb("mf_crow", [1, GT], F32) for _ in range(2)]
    cb = ctx.sb("mf_cb", [128, 512], BF16)
    gS = [ctx.sb("mf_gS", [128, GT], BF16) for _ in range(4)]
    uS = [ctx.sb("mf_uS", [128, GT], BF16) for _ in range(4)]
    hes = [ctx.sb("mf_hes", [128, GT], BF16) for _ in range(16)]
    hsh = [ctx.sb("mf_hsh", [128, GT], BF16) for _ in range(8)]
    yacc = [ctx.sb("mf_y", [128, D], F32) for _ in range(NT)]
    rb2 = [ctx.sb("mf_rb", [128, D], F32) for _ in range(2)]

    pG = [ctx.ps("mf_pG", [128, 512]) for _ in range(2)]
    pY = [ctx.ps("mf_pY", [128, 512]) for _ in range(2)]
    pT = ctx.ps("mf_pT", [128, 128])

    sx = ctx.pipe[8]
    sw = ctx.pipe[22]
    ssh = ctx.pipe[23]
    ss = ctx.pipe[10]
    sr = [ctx.pipe[14], ctx.pipe[15]]
    so = [ctx.pipe[16], ctx.pipe[17]]

    xnt_t = plan_load_xnt(S, sx, xnt, xn_ap, row0, nrows=GT)
    t_gw = S.dma("gpsimd", ss, gateT[:], w["gate_wT"][:, :])
    t_id = S.dma("gpsimd", ss, idf[:], cst["ID128"][:, :])
    t_gb = S.dma("gpsimd", ss, gbias[0:1, 0:32], w["gate_bias"][None, :])
    t_gw = t_id = t_gb
    S.op("vector", lambda e: e.memset(ones1f[:], 1.0))
    for i in range(NT):
        S.op("vector", lambda e, i=i: e.memset(yacc[i][:], 0.0))
    v_setup = S.inc("vector", lambda e: e.memset(sct[:, 0:1], 0.0), "dve")
    S.wait("tensor", ss, t_gb)
    S.wait("tensor", "dve", v_setup)
    pe_gb = S.inc("tensor", lambda e: e.matmul(pT[0:128, 0:32], ones1f[:],
                                               gbias[0:1, 0:32], start=True,
                                               stop=True), "pe")
    S.wait("vector", "pe", pe_gb)
    v_gb = S.inc("vector", lambda e: e.tensor_copy(gbias[:], pT[:, 0:32]), "dve")

    v_comb = v_gb
    a_tr_prev = 0
    for i in range(NT):
        if i == 0:
            S.wait("tensor", sx, xnt_t)
        S.wait("tensor", "dve", v_comb)
        for kc in range(8):
            fn = lambda e, kc=kc, i=i: e.matmul(
                pG[0][0:128, 0:32], xnt[kc][:, 128 * i:128 * (i + 1)],
                gateT[:, 32 * kc:32 * (kc + 1)],
                start=(kc == 0), stop=(kc == 7))
            if kc == 7:
                pe_lg = S.inc("tensor", fn, "pe")
            else:
                S.op("tensor", fn)
        S.wait("scalar", "pe", pe_lg)
        S.wait("scalar", "dve", v_comb)
        a_sg = S.inc("scalar", lambda e: e.activation(sct[:, 0:32], pG[0][:, 0:32],
                                                      AF.Sigmoid), "act")
        S.wait("vector", "act", a_sg)
        S.op("vector", lambda e: e.tensor_add(sct[:, 32:64], sct[:, 0:32], gbias[:]))
        sc4 = sct[:, 32:64].rearrange("p (g f) -> p g f", g=8)
        first = True
        for aa in range(4):
            for bb in range(aa + 1, 4):
                S.op("vector", lambda e, aa=aa, bb=bb: e.tensor_add(
                    sct[:, 72:80].rearrange("p (g o) -> p g o", o=1),
                    sc4[:, :, aa:aa + 1], sc4[:, :, bb:bb + 1]))
                if first:
                    S.op("vector", lambda e: e.tensor_copy(sct[:, 64:72],
                                                           sct[:, 72:80]))
                    first = False
                else:
                    S.op("vector", lambda e: e.tensor_max(sct[:, 64:72],
                                                          sct[:, 64:72],
                                                          sct[:, 72:80]))
        S.op("vector", lambda e: e.max(sct[:, 72:80], sct[:, 64:72]))
        S.op("vector", lambda e: e.tensor_scalar(sct[:, 64:72], sct[:, 64:72],
                                                 sct[:, 75:76], None, ALU.is_ge))
        S.op("vector", lambda e: e.tensor_tensor(
            sct[:, 32:64].rearrange("p (g f) -> p g f", g=8),
            sct[:, 32:64].rearrange("p (g f) -> p g f", g=8),
            sct[:, 64:72].rearrange("p (g o) -> p g o", o=1)
            .broadcast_to([128, 8, 4]), ALU.mult))
        S.op("vector", lambda e: e.max(sct[:, 72:80], sct[:, 32:64]))
        S.op("vector", lambda e: e.tensor_scalar(sct[:, 32:64], sct[:, 32:64],
                                                 sct[:, 79:80], None, ALU.is_ge))
        S.op("vector", lambda e: e.tensor_mul(sct[:, 32:64], sct[:, 32:64],
                                              sct[:, 0:32]))
        S.op("vector", lambda e: e.tensor_reduce(sct[:, 76:77], sct[:, 32:64],
                                                 AX.X, ALU.add))
        S.op("vector", lambda e: e.tensor_scalar_add(sct[:, 76:77], sct[:, 76:77],
                                                     1e-20))
        S.op("vector", lambda e: e.reciprocal(sct[:, 77:78], sct[:, 76:77]))
        v_comb = S.inc("vector", lambda e: e.tensor_scalar(
            comb[:], sct[:, 32:64], sct[:, 77:78], 2.5, ALU.mult, ALU.mult), "dve")
        S.wait("tensor", "dve", v_comb)
        S.wait("tensor", "act", a_tr_prev)
        pe_tr = S.inc("tensor", lambda e: e.transpose(pT[0:32, 0:128], comb[:],
                                                      idf[:]), "pe")
        S.wait("scalar", "pe", pe_tr)
        if i >= 1:
            S.wait("scalar", ctx.pipe[24], ct_st)
        a_tr_prev = S.inc("scalar", lambda e, i=i: e.copy(ctS[:], pT[0:32, 0:128]),
                          "act")
        S.wait("gpsimd", "act", a_tr_prev)
        ct_st = S.dma("gpsimd", ctx.pipe[24], ct_ap[0:32, 128 * i:128 * (i + 1)],
                      ctS[:])
    v_hs = 0
    a_ev = 0
    shw = [ctx.sb("mf_shw", [128, D], BF16) for _ in range(8)]
    t_shg = plan_load_w_chunks(S, ssh, shw, w["sh_gateT"])
    ev_sg = 0
    pe_sh = 0
    for ic in range(8):
        for g in range(GT // 512):
            idx = ic * (GT // 512) + g
            bank = pG[idx % 2]
            if idx >= 2:
                S.wait("tensor", "act", a_ev)
            if idx == 0:
                S.wait("tensor", ssh, t_shg)
                S.wait("tensor", "act", a_tr_prev)
            for kc in range(8):
                fn = lambda e, bank=bank, kc=kc, ic=ic, g=g: e.matmul(
                    bank[:], shw[kc][:, 128 * ic:128 * (ic + 1)],
                    xnt[kc][:, 512 * g:512 * (g + 1)],
                    start=(kc == 0), stop=(kc == 7))
                if kc == 7:
                    pe_sh = S.inc("tensor", fn, "pe")
                else:
                    S.op("tensor", fn)
            S.wait("scalar", "pe", pe_sh)
            a_ev = S.inc("scalar", lambda e, bank=bank, ic=ic, g=g: e.activation(
                hsh[ic][:, 512 * g:512 * (g + 1)], bank[:], AF.Silu), "act")
    S.wait("gpsimd", "pe", pe_sh)
    t_shu = plan_load_w_chunks(S, ssh, shw, w["sh_upT"])
    for ic in range(8):
        for g in range(GT // 512):
            idx = ic * (GT // 512) + g
            bank = pG[idx % 2]
            if idx >= 2:
                S.wait("tensor", "dve", v_hs)
            if idx == 0:
                S.wait("tensor", ssh, t_shu)
                S.wait("tensor", "act", a_ev)
            for kc in range(8):
                fn = lambda e, bank=bank, kc=kc, ic=ic, g=g: e.matmul(
                    bank[:], shw[kc][:, 128 * ic:128 * (ic + 1)],
                    xnt[kc][:, 512 * g:512 * (g + 1)],
                    start=(kc == 0), stop=(kc == 7))
                if kc == 7:
                    pe_sh = S.inc("tensor", fn, "pe")
                else:
                    S.op("tensor", fn)
            S.wait("vector", "pe", pe_sh)
            v_hs = S.inc("vector", lambda e, bank=bank, ic=ic, g=g: e.tensor_tensor(
                hsh[ic][:, 512 * g:512 * (g + 1)],
                hsh[ic][:, 512 * g:512 * (g + 1)], bank[:], ALU.mult), "dve")
    S.wait("gpsimd", "pe", pe_sh)
    t_shd = plan_load_w_chunks(S, ssh, shw, w["sh_downT"])
    v_y = 0
    pe_y = 0
    for i in range(NT):
        for half in range(2):
            bank = pY[(i * 2 + half) % 2]
            if i * 2 + half >= 2:
                S.wait("tensor", "dve", v_y)
            if i == 0 and half == 0:
                S.wait("tensor", ssh, t_shd)
                S.wait("tensor", "dve", v_hs)
            for ic in range(8):
                fn = lambda e, bank=bank, ic=ic, i=i, half=half: e.matmul(
                    bank[:], hsh[ic][:, 128 * i:128 * (i + 1)],
                    shw[ic][:, 512 * half:512 * (half + 1)],
                    start=(ic == 0), stop=(ic == 7))
                if ic == 7:
                    pe_y = S.inc("tensor", fn, "pe")
                else:
                    S.op("tensor", fn)
            S.wait("vector", "pe", pe_y)
            v_y = S.inc("vector", lambda e, bank=bank, i=i, half=half: e.tensor_add(
                yacc[i][:, 512 * half:512 * (half + 1)],
                yacc[i][:, 512 * half:512 * (half + 1)], bank[:]), "dve")

    cbS = ctx.sb("mf_cbS", [128, GT], BF16)
    scr_sem = [ctx.pipe[25], ctx.pipe[26]]
    crow_free = {0: 0, 1: 0}
    a_hes = 0
    v_hes = 0
    pe_dn = pe_y
    v_yd = v_y
    for eg in range(8):
        for j in range(4):
            e_ = 4 * eg + j
            S.wait("gpsimd", "pe", pe_dn)
            t_wg = 0
            for kc in range(8):
                t_wg = S.dma("gpsimd", sw, wgb[kc][:],
                             w["w_gateT"][1024 * e_ + 128 * kc:
                                          1024 * e_ + 128 * (kc + 1), :])
            for kc in range(8):
                t_wg = S.dma("gpsimd", sw, wub[kc][:],
                             w["w_upT"][1024 * e_ + 128 * kc:
                                        1024 * e_ + 128 * (kc + 1), :])
            for ic in range(4):
                t_wg = S.dma("gpsimd", sw, wdb[4 * j + ic][:],
                             w["w_downT"][512 * e_ + 128 * ic:
                                          512 * e_ + 128 * (ic + 1), :])
            calt = e_ % 2
            S.wait("sync", "dve", crow_free[calt])
            if e_ == 4 * eg:
                S.wait("sync", ctx.pipe[24], ct_st)
            t_cr = S.dma("sync", scr_sem[calt], crow[calt][:], ct_ap[e_:e_ + 1, :])
            S.wait("tensor", "dve", v_hes)
            S.wait("tensor", scr_sem[calt], t_cr)
            pe_cb = 0
            for g in range(GT // 512):
                pe_cb = S.inc("tensor", lambda e, g=g, calt=calt: e.matmul(
                    pG[g][:, 0:512], ones1f[:],
                    crow[calt][0:1, 512 * g:512 * (g + 1)],
                    start=True, stop=True), "pe")
            S.wait("vector", "pe", pe_cb)
            S.op("vector", lambda e: e.tensor_copy(cbS[:, 0:512], pG[0][:, 0:512]))
            v_cb = S.inc("vector", lambda e: e.tensor_copy(
                cbS[:, 512:1024], pG[1][:, 0:512]), "dve")
            crow_free[calt] = v_cb
            a_gu = 0
            for ic in range(4):
                for g in range(GT // 512):
                    idx = ic * 2 + g
                    bank = pG[idx % 2]
                    S.wait("tensor", "dve", v_cb if idx < 2 else v_cb)
                    if idx >= 2:
                        S.wait("tensor", "act", a_gu)
                    if idx == 0:
                        S.wait("tensor", sw, t_wg)
                    for kc in range(8):
                        fn = lambda e, bank=bank, kc=kc, ic=ic, g=g: e.matmul(
                            bank[:], wgb[kc][:, 128 * ic:128 * (ic + 1)],
                            xnt[kc][:, 512 * g:512 * (g + 1)],
                            start=(kc == 0), stop=(kc == 7))
                        if kc == 7:
                            pe_dn = S.inc("tensor", fn, "pe")
                        else:
                            S.op("tensor", fn)
                    S.wait("scalar", "pe", pe_dn)
                    a_gu = S.inc("scalar", lambda e, bank=bank, ic=ic, g=g: e.activation(
                        gS[ic][:, 512 * g:512 * (g + 1)], bank[:], AF.Silu), "act")
            for ic in range(4):
                for g in range(GT // 512):
                    idx = ic * 2 + g
                    bank = pG[idx % 2]
                    if True:
                        S.wait("tensor", "act", a_gu)
                    for kc in range(8):
                        fn = lambda e, bank=bank, kc=kc, ic=ic, g=g: e.matmul(
                            bank[:], wub[kc][:, 128 * ic:128 * (ic + 1)],
                            xnt[kc][:, 512 * g:512 * (g + 1)],
                            start=(kc == 0), stop=(kc == 7))
                        if kc == 7:
                            pe_dn = S.inc("tensor", fn, "pe")
                        else:
                            S.op("tensor", fn)
                    S.wait("scalar", "pe", pe_dn)
                    a_gu = S.inc("scalar", lambda e, bank=bank, ic=ic, g=g: e.copy(
                        uS[ic][:, 512 * g:512 * (g + 1)], bank[:]), "act")
            S.wait("vector", "act", a_gu)
            if eg > 0 or j > 0:
                S.wait("vector", "pe", pe_dn)
            for ic in range(4):
                S.op("vector", lambda e, ic=ic, j=j: e.tensor_mul(
                    hes[4 * j + ic][:], gS[ic][:], uS[ic][:]))
                v_hes = S.inc("vector", lambda e, ic=ic, j=j: e.tensor_mul(
                    hes[4 * j + ic][:], hes[4 * j + ic][:], cbS[:]), "dve")
        for i in range(NT):
            for half in range(2):
                bank = pY[(i * 2 + half) % 2]
                if i * 2 + half >= 2:
                    S.wait("tensor", "dve", v_yd)
                if i == 0 and half == 0:
                    S.wait("tensor", "dve", v_hes)
                    S.wait("tensor", sw, t_wg)
                n = 0
                for j in range(4):
                    for ic in range(4):
                        n += 1
                        fn = lambda e, bank=bank, j=j, ic=ic, i=i, half=half, \
                            first=(n == 1), last=(n == 16): e.matmul(
                            bank[:], hes[4 * j + ic][:, 128 * i:128 * (i + 1)],
                            wdb[4 * j + ic][:, 512 * half:512 * (half + 1)],
                            start=first, stop=last)
                        if n == 16:
                            pe_dn = S.inc("tensor", fn, "pe")
                        else:
                            S.op("tensor", fn)
                S.wait("vector", "pe", pe_dn)
                v_yd = S.inc("vector", lambda e, bank=bank, i=i, half=half:
                             e.tensor_add(yacc[i][:, 512 * half:512 * (half + 1)],
                                          yacc[i][:, 512 * half:512 * (half + 1)],
                                          bank[:]), "dve")

    vadd_t = {}
    store_t = {}
    for i in range(NT):
        i_abs = row0 + 128 * i
        alt = i % 2
        if i >= 2:
            S.wait("sync", "dve", vadd_t[i - 2])
        r_load = S.dma("sync", sr[alt], rb2[alt][:], r_in[i_abs:i_abs + 128, :])
        S.wait("vector", sr[alt], r_load)
        S.wait("vector", "dve", v_yd)
        if i >= 2:
            S.wait("vector", so[alt], store_t[i - 2])
        vadd_t[i] = S.inc("vector", lambda e, alt=alt, i=i: e.tensor_add(
            yacc[i][:], yacc[i][:], rb2[alt][:]), "dve")
        S.wait("gpsimd", "dve", vadd_t[i])
        store_t[i] = S.dma("gpsimd", so[alt], r_out[i_abs:i_abs + 128, :], yacc[i][:])


def prep_w_l1_ffn(inputs):
    w = {}
    gw = np.asarray(inputs["L1_ffn_gate_w"], dtype=np.float32)
    w["gate_wT"] = np.ascontiguousarray(
        gw.T.reshape(8, 128, 32).transpose(1, 0, 2).reshape(128, 256))
    w["gate_bias"] = np.asarray(inputs["L1_ffn_gate_bias"], dtype=np.float32)
    wg = np.asarray(inputs["L1_ffn_w_gate"], dtype=np.float32)
    w["w_gateT"] = np.ascontiguousarray(wg.transpose(0, 2, 1).reshape(32 * 1024, 512))
    wu = np.asarray(inputs["L1_ffn_w_up"], dtype=np.float32)
    w["w_upT"] = np.ascontiguousarray(wu.transpose(0, 2, 1).reshape(32 * 1024, 512))
    wd = np.asarray(inputs["L1_ffn_w_down"], dtype=np.float32)
    w["w_downT"] = np.ascontiguousarray(wd.transpose(0, 2, 1).reshape(32 * 512, 1024))
    w["sh_gateT"] = np.ascontiguousarray(
        np.asarray(inputs["L1_ffn_sh_gate"], dtype=np.float32).T)
    w["sh_upT"] = np.ascontiguousarray(
        np.asarray(inputs["L1_ffn_sh_up"], dtype=np.float32).T)
    w["sh_downT"] = np.ascontiguousarray(
        np.asarray(inputs["L1_ffn_sh_down"], dtype=np.float32).T)
    return w


def build_sigma6_program():
    es = ExitStack()
    nc = bass.Bass("TRN2", target_bir_lowering=False)
    rin = nc.dram_tensor("rin", [NTOK, D], F32, kind="ExternalInput").ap()
    rout = nc.dram_tensor("rout", [NTOK, D], F32, kind="ExternalOutput").ap()
    xn = nc.dram_tensor("xn", [NTOK, D], BF16, kind="Internal").ap()
    w = {}
    w["gate_wT"] = nc.dram_tensor("w_gate_wT", [128, 256], F32, kind="ExternalInput").ap()
    w["gate_bias"] = nc.dram_tensor("w_gate_bias", [32], F32, kind="ExternalInput").ap()
    w["w_gateT"] = nc.dram_tensor("w_w_gateT", [32 * 1024, 512], F32,
                                  kind="ExternalInput").ap()
    w["w_upT"] = nc.dram_tensor("w_w_upT", [32 * 1024, 512], F32,
                                kind="ExternalInput").ap()
    w["w_downT"] = nc.dram_tensor("w_w_downT", [32 * 512, 1024], F32,
                                  kind="ExternalInput").ap()
    for k in ("sh_gateT", "sh_upT", "sh_downT"):
        w[k] = nc.dram_tensor("w_" + k, [D, D], F32, kind="ExternalInput").ap()
    cst = {"ID128": nc.dram_tensor("c_ID128", [128, 128], F32,
                                   kind="ExternalInput").ap()}
    ct = nc.dram_tensor("ctbuf", [32, 1024], F32, kind="Internal").ap()
    ctx = Ctx(nc, es)
    S = Script(ctx)
    plan_l1_ffn(S, ctx, rin, rout, xn, ct, w, cst)
    for p in ctx.pipe:
        if p.v:
            S.wait("gpsimd", p, p.v)
    S.emit()
    es.close()
    return nc


CAP = 2048
NSLOT = 8 * CAP
GCAP = 8192.0


def plan_l0_ffn(S, ctx, r_in, r_out, xn_ap, xg_ap, h1_ap, yg_ap, cc_in, cc_out,
                w, cst, n_cores=8):
    ctx.slotuS = ctx.sb("mr_slotS", [128, 64], U32)
    with ExitStack() as es_ln:
        plan_ln(S, ctx.sub(es_ln), r_in, xn_ap, pipe0=0)
        S.barrier()
        S.emit()
    with ExitStack() as es_r:
        _l0ffn_route(S, ctx.sub(es_r), xn_ap, xg_ap, cc_in, cc_out, w, cst, n_cores)
        S.barrier()
        S.emit()
    for e_ in range(8):
        with ExitStack() as es_e:
            _l0ffn_expert(S, ctx.sub(es_e), xg_ap, h1_ap, yg_ap, w, e_)
            S.barrier()
            S.emit()
    with ExitStack() as es_f:
        _l0ffn_final(S, ctx.sub(es_f), r_in, r_out, yg_ap, cst)
        S.barrier()
        S.emit()


def _l0ffn_route(S, ctx, xn_ap, xg_ap, cc_in, cc_out, w, cst, n_cores):
    wr = ctx.sb("mr_wr", [128, 64], BF16)
    ones1f = ctx.sb("mr_o1f", [1, 128], F32)
    onesr8 = ctx.sb("mr_or8", [1, 8], F32)
    onescb = ctx.sb("mr_ocb", [128, 1], F32)
    lt128b = ctx.sb("mr_lt", [128, 128], F32)
    iota8 = ctx.sb("mr_iota", [128, 8], F32)
    c99 = ctx.sb("mr_c99", [128, 8], F32)
    iota8c = ctx.sb("mr_i8c", [8, 1], F32)
    id8 = ctx.sb("mr_id8", [8, 8], F32)
    pidS = ctx.sb("mr_pid", [1, 1], F32)
    lg = ctx.sb("mr_lg", [128, 8], F32)
    ex = ctx.sb("mr_ex", [128, 12], F32)
    ohS = ctx.sb("mr_oh", [128, 512], F32)
    prefrow = ctx.sb("mr_pref", [1, 24], F32)
    prefb = ctx.sb("mr_prefb", [1, 8], F32)
    myrankS = ctx.sb("mr_rank", [128, 64], F32)
    ptokS = ctx.sb("mr_ptok", [128, 64], F32)
    eidxS = ctx.sb("mr_eidx", [128, 64], F32)
    agS = ctx.sb("mr_ag", [8, 8], F32)
    mrow = ctx.sb("mr_mrow", [8, 2], F32)
    cprefb = ctx.sb("mr_cpref", [128, 8], F32)
    xnc = [ctx.sb("mr_xnc", [128, D], BF16) for _ in range(2)]
    slotf = ctx.sb("mr_slotf", [128, 8], F32)

    ccs = ctx.sb("mr_ccs", [128, 8], F32)

    pR = ctx.ps("mr_pR", [128, 512])
    pQ = ctx.ps("mr_pQ", [128, 512])

    sx = ctx.pipe[8]
    ss = ctx.pipe[10]
    sxt = [ctx.pipe[14], ctx.pipe[15]]
    sgo = [ctx.pipe[16], ctx.pipe[17]]
    scc = ctx.pipe[27]

    zt = ctx.sb("mr_zt", [128, D], BF16)
    v_z = S.inc("vector", lambda e: e.memset(zt[:], 0.0), "dve")
    S.wait("gpsimd", "dve", v_z)
    szg = ctx.pipe[28]
    for j in range(NSLOT // 128):
        t_zg = S.dma("gpsimd", szg, xg_ap[128 * j:128 * (j + 1), :], zt[:])
    t_wr = S.dma("gpsimd", ss, wr[:], w["WrC"][:, :])
    t_lt = S.dma("gpsimd", ss, lt128b[:], cst["lt128"][:, :])
    t_io = S.dma("gpsimd", ss, iota8[:], cst["iota8"][:, :])
    t_c99 = S.dma("gpsimd", ss, c99[:], cst["c99m"][:, :])
    t_i8c = S.dma("gpsimd", ss, iota8c[:], cst["iota8col"][:, :])
    t_id8 = S.dma("gpsimd", ss, id8[:], cst["ID8"][:, :])
    t_pid = S.dma("gpsimd", ss, pidS[:], w["core_id"][:, :])
    t_all = t_pid
    S.op("vector", lambda e: e.memset(ones1f[:], 1.0))
    S.op("vector", lambda e: e.memset(onesr8[:], 1.0))
    S.op("vector", lambda e: e.memset(onescb[:], 1.0))
    S.op("vector", lambda e: e.memset(prefrow[:], 0.0))
    S.op("vector", lambda e: e.memset(ccs[:], 0.0))
    S.op("vector", lambda e: e.memset(agS[:], 0.0))
    v_setup = S.inc("vector", lambda e: e.memset(lg[:], 0.0), "dve")

    v_rank = {}
    a_ev = 0
    prev_xnt_t = 0
    prev_read = 0
    for gi in range(4):
        with ExitStack() as es_g:
            sub = ctx.sub(es_g)
            xnt = [sub.sb("mr_xnt", [128, 2048], BF16) for _ in range(8)]
            if gi > 0:
                S.wait("sync", sx, prev_xnt_t)
                S.wait("sync", "pe", prev_read)
            xnt_t = plan_load_xnt(S, sx, xnt, xn_ap, 2048 * gi, nrows=2048)
            for ti in range(16):
                i = gi * 16 + ti
                if ti == 0:
                    S.wait("tensor", sx, xnt_t)
                    S.wait("tensor", ss, t_all)
                    S.wait("tensor", "dve", v_setup)
                S.wait("tensor", "dve", v_rank.get(i - 1, 0))
                for kc in range(8):
                    fn = lambda e, kc=kc, ti=ti, xnt=xnt: e.matmul(
                        pR[0:128, 0:8], xnt[kc][:, 128 * ti:128 * (ti + 1)],
                        wr[:, 8 * kc:8 * (kc + 1)], start=(kc == 0), stop=(kc == 7))
                    if kc == 7:
                        pe_lg = S.inc("tensor", fn, "pe")
                    else:
                        S.op("tensor", fn)
                S.wait("vector", "pe", pe_lg)
                v_lg = S.inc("vector", lambda e: e.tensor_copy(lg[:], pR[:, 0:8]), "dve")
                S.op("vector", lambda e: e.reduce_max(ex[:, 0:1], lg[:], AX.X))
                v_m = S.inc("vector", lambda e: e.tensor_scalar_mul(
                    ex[:, 1:2], ex[:, 0:1], -1.0), "dve")
                S.wait("scalar", "dve", v_m)
                a_ev = S.inc("scalar", lambda e: e.activation(
                    ex[:, 4:12], lg[:], AF.Exp, bias=ex[:, 1:2],
                    accum_out=ex[:, 2:3]), "act")
                S.wait("vector", "act", a_ev)
                S.op("vector", lambda e: e.reciprocal(ex[:, 3:4], ex[:, 2:3]))
                S.op("vector", lambda e, i=i: e.tensor_copy(ptokS[:, i:i + 1],
                                                            ex[:, 3:4]))
                S.op("vector", lambda e: e.tensor_scalar(ex[:, 4:12], lg[:],
                                                         ex[:, 0:1], None,
                                                         ALU.is_ge))
                S.op("vector", lambda e: e.tensor_mul(ex[:, 4:12], ex[:, 4:12],
                                                      c99[:]))
                S.op("vector", lambda e: e.tensor_scalar(ex[:, 4:12], ex[:, 4:12],
                                                         -1.0, 99.0, ALU.mult,
                                                         ALU.add))
                S.op("vector", lambda e: e.tensor_reduce(ex[:, 0:1], ex[:, 4:12],
                                                         AX.X, ALU.min))
                S.op("vector", lambda e, i=i: e.tensor_copy(eidxS[:, i:i + 1],
                                                            ex[:, 0:1]))
                S.op("vector", lambda e, i=i: e.tensor_scalar(
                    ohS[:, 8 * i:8 * (i + 1)], iota8[:], ex[:, 0:1], None,
                    ALU.is_equal))
                S.wait("vector", "act", a_ev)
                v_pb = S.inc("vector", lambda e: e.tensor_copy(
                    prefb[:], prefrow[0:1, 0:8]), "dve")
                S.wait("tensor", "dve", v_pb)
                S.op("tensor", lambda e, i=i: e.matmul(
                    pR[0:128, 16:24], lt128b[:], ohS[:, 8 * i:8 * (i + 1)],
                    start=True, stop=False))
                pe_rk = S.inc("tensor", lambda e: e.matmul(
                    pR[0:128, 16:24], ones1f[:], prefb[:], start=False,
                    stop=True), "pe")
                pe_ct = S.inc("tensor", lambda e, i=i: e.matmul(
                    pR[0:1, 32:40], onescb[:], ohS[:, 8 * i:8 * (i + 1)],
                    start=True, stop=True), "pe")
                S.wait("vector", "pe", pe_ct)
                S.op("vector", lambda e, i=i: e.tensor_mul(
                    ex[:, 4:12], pR[:, 16:24], ohS[:, 8 * i:8 * (i + 1)]))
                S.op("vector", lambda e, i=i: e.tensor_reduce(
                    myrankS[:, i:i + 1], ex[:, 4:12], AX.X, ALU.add))
                v_rank[i] = S.inc("vector", lambda e: e.tensor_add(
                    prefrow[0:1, 0:8], prefrow[0:1, 0:8], pR[0:1, 32:40]), "dve")
            prev_xnt_t = xnt_t
            prev_read = pe_lg
    S.wait("vector", "dve", v_rank[63])
    v_cc = S.inc("vector", lambda e: e.tensor_copy(ccs[0:1, 0:8],
                                                   prefrow[0:1, 0:8]), "dve")
    S.wait("gpsimd", "dve", v_cc)
    t_ccin = S.dma("gpsimd", scc, cc_in[0:128, 0:8], ccs[:])
    S.wait("gpsimd", scc, t_ccin)
    if n_cores == 1:
        cc_t0 = S.dma("gpsimd", scc, cc_out[0:128, 0:8], cc_in[0:128, 0:8])
        S.wait("gpsimd", scc, cc_t0)
        cc_t = S.inc("gpsimd", lambda e: e.engine_nop(), "gp", 1)
    else:
        cc_t = S.inc("gpsimd", lambda e: e.collective_compute(
            "AllGather", ALU.bypass, replica_groups=[list(range(n_cores))],
            ins=[cc_in.opt()], outs=[cc_out.opt()]), "gp", 1)
    S.wait("sync", "gp", cc_t)
    sag = ctx.pipe[29]
    ag_t = S.dma("sync", sag, agS[0:n_cores, 0:8],
                 cc_out.rearrange("(r p) e -> r (p e)", p=128)[0:n_cores, 0:8])
    S.wait("tensor", ss, t_all)
    pe_pb = S.inc("tensor", lambda e: e.matmul(pQ[0:8, 0:1], onesr8[:], pidS[:],
                                               start=True, stop=True), "pe")
    S.wait("vector", "pe", pe_pb)
    S.op("vector", lambda e: e.tensor_copy(mrow[0:8, 0:1], pQ[0:8, 0:1]))
    v_mk = S.inc("vector", lambda e: e.tensor_tensor(mrow[0:8, 1:2], iota8c[:],
                                                     mrow[0:8, 0:1], ALU.is_lt),
                 "dve")
    S.wait("sync", sag, ag_t)
    S.wait("tensor", sag, ag_t)
    S.wait("tensor", "dve", v_mk)
    pe_cp = S.inc("tensor", lambda e: e.matmul(pQ[0:8, 2:3], agS[0:8, 0:8],
                                               mrow[0:8, 1:2], start=True,
                                               stop=True), "pe")
    S.wait("vector", "pe", pe_cp)
    v_cpc = S.inc("vector", lambda e: e.tensor_copy(mrow[0:8, 0:1], pQ[0:8, 2:3]),
                  "dve")
    S.wait("tensor", "dve", v_cpc)
    pe_cr = S.inc("tensor", lambda e: e.matmul(pQ[0:1, 8:16], mrow[0:8, 0:1],
                                               id8[:], start=True, stop=True), "pe")
    S.wait("vector", "pe", pe_cr)
    v_crow = S.inc("vector", lambda e: e.tensor_copy(prefrow[0:1, 8:16],
                                                     pQ[0:1, 8:16]), "dve")
    S.wait("tensor", "dve", v_crow)
    pe_cb = S.inc("tensor", lambda e: e.matmul(pQ[0:128, 16:24], ones1f[:],
                                               prefrow[0:1, 8:16], start=True,
                                               stop=True), "pe")
    S.wait("vector", "pe", pe_cb)
    v_cpb = S.inc("vector", lambda e: e.tensor_copy(cprefb[:], pQ[:, 16:24]), "dve")

    sc_store = {}
    for i in range(64):
        alt = i % 2
        if i >= 2:
            S.wait("sync", sgo[alt], sc_store.get(i - 2, 0))

        t_x = S.dma("sync", sxt[alt], xnc[alt][:], xn_ap[128 * i:128 * (i + 1), :])
        S.wait("vector", sxt[alt], t_x)
        S.wait("vector", "dve", v_cpb)
        S.op("vector", lambda e, i=i: e.tensor_mul(ex[:, 4:12],
                                                   ohS[:, 8 * i:8 * (i + 1)],
                                                   cprefb[:]))
        S.op("vector", lambda e: e.tensor_reduce(ex[:, 0:1], ex[:, 4:12], AX.X,
                                                 ALU.add))
        S.op("vector", lambda e, i=i: e.tensor_add(ex[:, 0:1], ex[:, 0:1],
                                                   myrankS[:, i:i + 1]))
        S.op("vector", lambda e: e.tensor_scalar(ex[:, 1:2], ex[:, 0:1], GCAP,
                                                 None, ALU.is_lt))
        S.op("vector", lambda e, i=i: e.tensor_scalar(ex[:, 2:3],
                                                      myrankS[:, i:i + 1],
                                                      float(CAP), None, ALU.is_lt))
        S.op("vector", lambda e: e.tensor_mul(ex[:, 1:2], ex[:, 1:2], ex[:, 2:3]))
        S.op("vector", lambda e, i=i: e.tensor_scalar(ex[:, 3:4], eidxS[:, i:i + 1],
                                                      float(CAP), None, ALU.mult))
        S.op("vector", lambda e, i=i: e.tensor_add(ex[:, 3:4], ex[:, 3:4],
                                                   myrankS[:, i:i + 1]))
        S.op("vector", lambda e: e.tensor_scalar_add(ex[:, 3:4], ex[:, 3:4],
                                                     -65536.0))
        S.op("vector", lambda e: e.tensor_mul(ex[:, 3:4], ex[:, 3:4], ex[:, 1:2]))
        S.op("vector", lambda e: e.tensor_scalar_add(ex[:, 3:4], ex[:, 3:4],
                                                     65536.0))
        prev = sc_store.get(i - 2, 0)
        S.wait("vector", sgo[alt], prev)
        v_su = S.inc("vector", lambda e, i=i: e.tensor_copy(
            ctx.slotuS[:, i:i + 1], ex[:, 3:4]), "dve")
        S.op("vector", lambda e, i=i: e.tensor_mul(ex[:, 2:3], ptokS[:, i:i + 1],
                                                   ex[:, 1:2]))
        v_xc = S.inc("vector", lambda e, alt=alt: e.tensor_scalar(
            xnc[alt][:], xnc[alt][:], ex[:, 2:3], None, ALU.mult), "dve")
        S.wait("gpsimd", "dve", v_xc)
        if i == 0:
            S.wait("gpsimd", szg, t_zg)
        S.wait("gpsimd", sgo[1 - alt], sc_store.get(i - 1, 0))
        sc_store[i] = S.inc("gpsimd", lambda e, alt=alt, i=i: e.indirect_dma_start(
            xg_ap[:, :], IndirectOffsetOnAxis(ctx.slotuS[:, i:i + 1], 0),
            xnc[alt][:], None, bounds_check=ctx.bc_reg(e, NSLOT - 1),
            oob_is_err=False), sgo[alt], 16)


def _l0ffn_expert(S, ctx, xg_ap, h1_ap, yg_ap, w, e_):
    xgT = [ctx.sb("me_xgT", [128, CAP], BF16) for _ in range(8)]
    w1b = [ctx.sb("me_w1", [128, 512], BF16) for _ in range(8)]
    h1t = [ctx.sb("me_h1", [128, CAP], BF16) for _ in range(8)]
    w2q = [ctx.sb("me_w2", [128, D], BF16) for _ in range(8)]
    yS = [ctx.sb("me_y", [128, D], F32) for _ in range(16)]
    pA = ctx.ps("me_pA", [128, 512])
    pB = ctx.ps("me_pB", [128, 512])
    sx = ctx.pipe[8]
    sw1 = ctx.pipe[9]
    sw2 = ctx.pipe[19]
    syo = [ctx.pipe[16], ctx.pipe[17]]

    xg_t = plan_load_xnt(S, sx, xgT, xg_ap, CAP * e_, nrows=CAP)
    for i in range(16):
        S.op("vector", lambda e, i=i: e.memset(yS[i][:], 0.0))
    v_y0 = S.inc("vector", lambda e: e.memset(h1t[0][0:1, 0:1], 0.0), "dve")

    a_h1 = 0
    pe_h1 = 0
    pe_y = 0
    v_y = v_y0
    for fcg in range(4):
        for sub_ in range(2):
            fg = 2 * fcg + sub_
            t_w1 = 0
            S.wait("gpsimd", "pe", pe_h1)
            for kc in range(8):
                t_w1 = S.dma("gpsimd", sw1, w1b[kc][:],
                             w["W1T"][1024 * e_ + 128 * kc:
                                      1024 * e_ + 128 * (kc + 1),
                                      512 * fg:512 * (fg + 1)])
            for fc4 in range(4):
                for sg in range(CAP // 512):
                    idx = fc4 * (CAP // 512) + sg
                    bank = (pA, pB)[idx % 2]
                    S.wait("tensor", "act", a_h1)
                    if fcg == 0 and sub_ == 0 and idx == 0:
                        S.wait("tensor", sx, xg_t)
                        S.wait("tensor", "dve", v_y0)
                    if idx == 0:
                        S.wait("tensor", sw1, t_w1)
                        S.wait("tensor", "dve", v_y)
                    for kc in range(8):
                        fn = lambda e, bank=bank, kc=kc, fc4=fc4, sg=sg: e.matmul(
                            bank[:], w1b[kc][:, 128 * fc4:128 * (fc4 + 1)],
                            xgT[kc][:, 512 * sg:512 * (sg + 1)],
                            start=(kc == 0), stop=(kc == 7))
                        if kc == 7:
                            pe_h1 = S.inc("tensor", fn, "pe")
                        else:
                            S.op("tensor", fn)
                    S.wait("scalar", "pe", pe_h1)
                    a_h1 = S.inc("scalar", lambda e, bank=bank, sub_=sub_, fc4=fc4,
                                 sg=sg: e.activation(
                        h1t[sub_ * 4 + fc4][:, 512 * sg:512 * (sg + 1)], bank[:],
                        AF.Relu), "act")
        t_w2 = 0
        S.wait("gpsimd", "pe", pe_y)
        for q in range(8):
            fcabs = fcg * 8 + q
            t_w2 = S.dma("gpsimd", sw2, w2q[q][:],
                         w["W2"][4096 * e_ + 128 * fcabs:
                                 4096 * e_ + 128 * (fcabs + 1), :])
        for i in range(16):
            for half in range(2):
                idx = i * 2 + half
                bank = (pA, pB)[idx % 2]
                if idx >= 2:
                    S.wait("tensor", "dve", v_y)
                if idx == 0:
                    S.wait("tensor", sw2, t_w2)
                    S.wait("tensor", "act", a_h1)
                for q in range(8):
                    fn = lambda e, bank=bank, q=q, i=i, half=half: e.matmul(
                        bank[:], h1t[q][:, 128 * i:128 * (i + 1)],
                        w2q[q][:, 512 * half:512 * (half + 1)],
                        start=(q == 0), stop=(q == 7))
                    if q == 7:
                        pe_y = S.inc("tensor", fn, "pe")
                    else:
                        S.op("tensor", fn)
                S.wait("vector", "pe", pe_y)
                v_y = S.inc("vector", lambda e, bank=bank, i=i, half=half:
                            e.tensor_add(yS[i][:, 512 * half:512 * (half + 1)],
                                         yS[i][:, 512 * half:512 * (half + 1)],
                                         bank[:]), "dve")
    st = {}
    for i in range(16):
        S.wait("gpsimd", "dve", v_y)
        st[i] = S.dma("gpsimd", syo[i % 2],
                      yg_ap[CAP * e_ + 128 * i:CAP * e_ + 128 * (i + 1), :],
                      yS[i][:])


def _l0ffn_final(S, ctx, r_in, r_out, yg_ap, cst):
    yg = [ctx.sb("mg_yg", [128, D], F32) for _ in range(2)]
    rb2 = [ctx.sb("mg_rb", [128, D], F32) for _ in range(2)]
    slotu = ctx.slotuS
    sr = [ctx.pipe[14], ctx.pipe[15]]
    sg = [ctx.pipe[19], ctx.pipe[20]]
    so = [ctx.pipe[16], ctx.pipe[17]]
    ssl = ctx.pipe[27]
    t_sl = 0
    vadd = {}
    stt = {}
    for i in range(64):
        alt = i % 2
        if i >= 2:
            S.wait("sync", "dve", vadd[i - 2])
            S.wait("sync", so[alt], stt[i - 2])
            S.wait("gpsimd", "dve", vadd[i - 2])
        r_load = S.dma("sync", sr[alt], rb2[alt][:], r_in[128 * i:128 * (i + 1), :])
        S.wait("gpsimd", "dve", vadd.get(i - 2, 0))
        gz = S.inc("gpsimd", lambda e, alt=alt: e.memset(yg[alt][:], 0.0), "gp")
        S.wait("gpsimd", "gp", gz)
        g_t = S.inc("gpsimd", lambda e, alt=alt, i=i: e.indirect_dma_start(
            yg[alt][:], None, yg_ap[:, :],
            IndirectOffsetOnAxis(slotu[:, i:i + 1], 0),
            bounds_check=ctx.bc_reg(e, NSLOT - 1), oob_is_err=False), sg[alt], 16)
        S.wait("vector", sg[alt], g_t)
        S.wait("vector", sr[alt], r_load)
        if i >= 2:
            S.wait("vector", so[alt], stt[i - 2])
        vadd[i] = S.inc("vector", lambda e, alt=alt: e.tensor_add(
            rb2[alt][:], rb2[alt][:], yg[alt][:]), "dve")
        S.wait("gpsimd", "dve", vadd[i])
        stt[i] = S.dma("gpsimd", so[alt], r_out[128 * i:128 * (i + 1), :],
                       rb2[alt][:])


def prep_w_l0_ffn(inputs, core_id):
    w = {}
    wr = np.asarray(inputs["L0_ffn_Wr"], dtype=np.float32)
    w["WrC"] = np.ascontiguousarray(
        wr.reshape(8, 128, 8).transpose(1, 0, 2).reshape(128, 64))
    W1 = np.asarray(inputs["L0_ffn_W1"], dtype=np.float32)
    w["W1T"] = np.ascontiguousarray(W1.reshape(8 * 1024, 4096))
    W2 = np.asarray(inputs["L0_ffn_W2"], dtype=np.float32)
    w["W2"] = np.ascontiguousarray(W2.reshape(8 * 4096, 1024))
    w["core_id"] = np.array([[float(core_id)]], dtype=np.float32)
    return w


def consts_l0_ffn():
    c = {}
    c["c99m"] = np.tile(99.0 - np.arange(8, dtype=np.float32), (128, 1))
    c["iota8col"] = np.arange(8, dtype=np.float32).reshape(8, 1)
    c["ID8"] = np.eye(8, dtype=np.float32)
    return c


def build_sigma2_program(n_cores=1):
    es = ExitStack()
    nc = bass.Bass("TRN2", target_bir_lowering=False)
    rin = nc.dram_tensor("rin", [NTOK, D], F32, kind="ExternalInput").ap()
    rout = nc.dram_tensor("rout", [NTOK, D], F32, kind="ExternalOutput").ap()
    xn = nc.dram_tensor("xn", [NTOK, D], BF16, kind="Internal").ap()
    xg = nc.dram_tensor("xgbuf", [NSLOT, D], BF16, kind="Internal").ap()
    yg = nc.dram_tensor("ygbuf", [NSLOT, D], F32, kind="Internal").ap()
    slot = nc.dram_tensor("slotbuf", [64, 128], U32, kind="Internal").ap()
    cc_in = nc.dram_tensor("cc_in2", [128, 8], F32, kind="Internal").ap()
    cc_out = nc.dram_tensor("cc_out2", [128 * n_cores, 8], F32, kind="Internal").ap()
    w = {}
    w["WrC"] = nc.dram_tensor("w_WrC", [128, 64], F32, kind="ExternalInput").ap()
    w["W1T"] = nc.dram_tensor("w_W1T", [8 * 1024, 4096], F32, kind="ExternalInput").ap()
    w["W2"] = nc.dram_tensor("w_W2", [8 * 4096, 1024], F32, kind="ExternalInput").ap()
    w["core_id"] = nc.dram_tensor("w_core_id", [1, 1], F32, kind="ExternalInput").ap()
    cst = {}
    for k, shp in (("lt128", [128, 128]), ("iota8", [128, 8]), ("c99m", [128, 8]),
                   ("iota8col", [8, 1]), ("ID8", [8, 8])):
        cst[k] = nc.dram_tensor("c_" + k, shp, F32, kind="ExternalInput").ap()
    ctx = Ctx(nc, es)
    ctx.slot_ap = slot
    S = Script(ctx)
    plan_l0_ffn(S, ctx, rin, rout, xn, xg, None, yg, cc_in, cc_out, w, cst,
                n_cores=n_cores)
    for p in ctx.pipe:
        if p.v:
            S.wait("gpsimd", p, p.v)
    S.emit()
    es.close()
    return nc
